# version 01
# baseline (speedup 1.0000x reference)
import numpy as np
import concourse.bass as bass
import concourse.mybir as mybir

F32 = mybir.dt.float32
BF16 = mybir.dt.bfloat16
AF = mybir.ActivationFunctionType
ALU = mybir.AluOpType

NCORES = 8


class Phase:
    def __init__(self, nc):
        self.nc = nc
        self.eng = {
            "sync": nc.sync,
            "tensor": nc.tensor,
            "vector": nc.vector,
            "scalar": nc.scalar,
            "gpsimd": nc.gpsimd,
        }
        self._ctxs = []
        self.bar_sem = self._sem("bar")
        self.nbar = 0

    def _sem(self, name):
        ctx = self.nc.semaphore(name)
        h = ctx.__enter__()
        self._ctxs.append(ctx)
        return h

    def barrier(self):
        self.nbar += 1
        target = 5 * self.nbar
        for e in self.eng.values():
            e.nop().then_inc(self.bar_sem, 1)
        for e in self.eng.values():
            e.wait_ge(self.bar_sem, target)

    def close(self):
        for ctx in reversed(self._ctxs):
            ctx.__exit__(None, None, None)


class Chain:

    def __init__(self, ph, name):
        self.sem = ph._sem(name)
        self.n = 0

    def inc(self, inst, k=1):
        inst.then_inc(self.sem, k)
        self.n += k
        return self.n

    def wait(self, eng, n=None):
        eng.wait_ge(self.sem, self.n if n is None else n)


class DQ:

    def __init__(self, ph, eng_name, name=None):
        self.e = ph.eng[eng_name]
        self.sem = ph._sem(name or ("dq_" + eng_name))
        self.n = 0

    def start(self, out, in_, serialize=True, **kw):
        if serialize and self.n:
            self.e.wait_ge(self.sem, self.n)
        self.e.dma_start(out=out, in_=in_, **kw).then_inc(self.sem, 16)
        self.n += 16
        return self.n

    def wait_all(self, eng):
        eng.wait_ge(self.sem, self.n)


def ceil_div(a, b):
    return (a + b - 1) // b


def ln_stats(ph, ch, slabs, tcols, ps_a, ps_b, sq_sb, ones_sb,
             mu_row, rstd_row, scr_row, eps_ap=None, nfeat=None):
    nkt = len(slabs)
    nfeat = nfeat if nfeat is not None else nkt * 128
    t, v, s = ph.eng["tensor"], ph.eng["vector"], ph.eng["scalar"]
    for kt in range(nkt):
        i = t.matmul(ps_a[0:1, 0:tcols], ones_sb[:, 0:1], slabs[kt],
                     start=(kt == 0), stop=(kt == nkt - 1))
    ch.inc(i)
    for kt in range(nkt):
        ch.wait(s)
        ch.inc(s.activation(out=sq_sb[:, 0:tcols], in_=slabs[kt],
                            func=AF.Square))
        ch.wait(t)
        ch.inc(t.matmul(ps_b[0:1, 0:tcols], ones_sb[:, 0:1], sq_sb[:, 0:tcols],
                        start=(kt == 0), stop=(kt == nkt - 1)))
        if kt < nkt - 1:
            ch.wait(s)
    ch.wait(v)
    inv_n = 1.0 / float(nfeat)
    mu = mu_row[0:1, 0:tcols]
    rs = rstd_row[0:1, 0:tcols]
    sc = scr_row[0:1, 0:tcols]
    ch.inc(v.tensor_scalar(mu, ps_a[0:1, 0:tcols], inv_n, None, ALU.mult))
    ch.wait(v)
    ch.inc(v.tensor_tensor(sc, mu, mu, ALU.mult))
    ch.wait(v)
    ch.inc(v.scalar_tensor_tensor(rs, ps_b[0:1, 0:tcols], inv_n, sc, ALU.mult,
                                  ALU.subtract))
    ch.wait(s)
    ch.inc(s.activation(out=sc, in_=rs, func=AF.Sqrt, bias=eps_ap))
    ch.wait(v)
    ch.inc(v.reciprocal(rs, sc))


def bcast_rows(ph, ch, rows_and_ps, tcols, ones_row):
    t = ph.eng["tensor"]
    ch.wait(t)
    for row, ps in rows_and_ps:
        i = t.matmul(ps[:, 0:tcols], ones_row[0:1, 0:128], row[0:1, 0:tcols],
                     start=True, stop=True)
    ch.inc(i)



def pack_vec(v, nblk):
    return np.ascontiguousarray(v.reshape(nblk, 128).T).astype(np.float32)



def emit_l1ffn(nc, T, TC=512):
    assert T % TC == 0
    NKT = 8
    HB = 32
    ph = Phase(nc)
    x3 = nc.dram_tensor("x3T", [1024, T], F32, kind="ExternalInput").ap()
    w1 = nc.dram_tensor("w1e", [1024, 4096], F32, kind="ExternalInput").ap()
    w2 = nc.dram_tensor("w2e", [4096, 1024], F32, kind="ExternalInput").ap()
    b1d = nc.dram_tensor("b1d", [128, 32], F32, kind="ExternalInput").ap()
    b2d = nc.dram_tensor("b2d", [128, 8], F32, kind="ExternalInput").ap()
    gmd = nc.dram_tensor("gmd", [128, 8], F32, kind="ExternalInput").ap()
    btd = nc.dram_tensor("btd", [128, 8], F32, kind="ExternalInput").ap()
    onesd = nc.dram_tensor("onesd", [128, 128], F32, kind="ExternalInput").ap()
    constsd = nc.dram_tensor("constsd", [1, 8], F32, kind="ExternalInput").ap()
    x4 = nc.dram_tensor("x4T", [1024, T], F32, kind="ExternalOutput").ap()

    w1r = w1.rearrange("(kt p) o -> p kt o", p=128)

    import contextlib
    with contextlib.ExitStack() as _es:
        x3_sb = _es.enter_context(nc.sbuf_tensor("x3_sb", [128, NKT * T], F32))
        tgt_bf = _es.enter_context(nc.sbuf_tensor("tgt_bf", [128, NKT * TC], BF16))
        h_bf = _es.enter_context(nc.sbuf_tensor("h_bf", [128, HB * TC], BF16))
        eo_sb = _es.enter_context(nc.sbuf_tensor("eo_sb", [128, NKT * TC], F32))
        xi_sb = _es.enter_context(nc.sbuf_tensor("xi_sb", [128, NKT * TC], F32))
        w1buf = _es.enter_context(nc.sbuf_tensor("w1buf", [128, 2 * NKT * 128], BF16))
        w2buf = _es.enter_context(nc.sbuf_tensor("w2buf", [128, 2 * 4 * 128], BF16))
        scr1 = _es.enter_context(nc.sbuf_tensor("scr1", [128, TC], F32))
        scr2 = _es.enter_context(nc.sbuf_tensor("scr2", [128, TC], F32))
        scr3 = _es.enter_context(nc.sbuf_tensor("scr3", [128, TC], F32))
        bias_sb = _es.enter_context(nc.sbuf_tensor("bias_sb", [128, 64], F32))
        ones_row = _es.enter_context(nc.sbuf_tensor("ones_row", [1, 128], F32))
        rows = _es.enter_context(nc.sbuf_tensor("rows", [1, 8 * TC], F32))
        p0 = _es.enter_context(nc.psum_tensor("p0", [128, TC], F32))
        p1 = _es.enter_context(nc.psum_tensor("p1", [128, TC], F32))
        p2 = _es.enter_context(nc.psum_tensor("p2", [128, TC], F32))
        p3 = _es.enter_context(nc.psum_tensor("p3", [128, TC], F32))
        p4 = _es.enter_context(nc.psum_tensor("p4", [128, TC], F32))
        p5 = _es.enter_context(nc.psum_tensor("p5", [128, TC], F32))
        p6 = _es.enter_context(nc.psum_tensor("p6", [128, TC], F32))
        p7 = _es.enter_context(nc.psum_tensor("p7", [128, TC], F32))
        _dummy = 0
        ps_h = [p0, p1]
        ps_eo = [p2, p3, p4, p5]
        b1_sb = bias_sb[:, 0:32]
        b2_sb = bias_sb[:, 32:40]
        gm_sb = bias_sb[:, 40:48]
        bt_sb = bias_sb[:, 48:56]
        ones_col = bias_sb[:, 56:57]
        eps_ap = bias_sb[0:1, 57:58]
        mu1, rs1 = rows[0:1, 0*TC:1*TC], rows[0:1, 1*TC:2*TC]
        mu2, rs2 = rows[0:1, 2*TC:3*TC], rows[0:1, 3*TC:4*TC]
        mu3, rs3 = rows[0:1, 4*TC:5*TC], rows[0:1, 5*TC:6*TC]
        scrow = rows[0:1, 6*TC:7*TC]

        ld = DQ(ph, "sync", "ld")
        st = DQ(ph, "sync", "st")
        wq = [DQ(ph, "gpsimd", "wq0"), DQ(ph, "gpsimd", "wq1")]
        ch = Chain(ph, "ch")
        mw = Chain(ph, "mw")
        mh = Chain(ph, "mh")
        eh = Chain(ph, "eh")

        t, v, s, g, sy = (ph.eng["tensor"], ph.eng["vector"], ph.eng["scalar"],
                          ph.eng["gpsimd"], ph.eng["sync"])

        ld.start(x3_sb[:, :].rearrange("p (kt t) -> p kt t", kt=NKT),
                 x3.rearrange("(kt p) t -> p kt t", p=128))
        ld.start(b1_sb, b1d[:, :])
        ld.start(b2_sb, b2d[:, :])
        ld.start(gm_sb, gmd[:, :])
        ld.start(bt_sb, btd[:, :])
        ld.start(ones_col, onesd[:, 0:1], allow_slow_non_contiguous=True)
        ld.start(ones_row[0:1, :], onesd[0:1, :])
        ld.start(bias_sb[0:1, 57:58], constsd[0:1, 0:1], allow_slow_non_contiguous=True)
        for e in ph.eng.values():
            ld.wait_all(e)

        NTC = T // TC
        for tc in range(NTC):
            t0 = tc * TC
            x3kt = lambda kt: x3_sb[:, kt * T + t0: kt * T + t0 + TC]

            ch.wait(t); ch.wait(s)
            ln_stats(ph, ch, [x3kt(kt) for kt in range(NKT)], TC, p6, p7,
                     scr1, ones_col, mu1, rs1, scrow, eps_ap=eps_ap)
            bcast_rows(ph, ch, [(mu1, p6), (rs1, p7)], TC, ones_row)
            ch.wait(v)
            for kt in range(NKT):
                ch.wait(v)
                ch.inc(v.tensor_tensor(scr1[:, 0:TC], x3kt(kt), p6[:, 0:TC],
                                       ALU.subtract))
                ch.wait(v)
                ch.inc(v.tensor_tensor(tgt_bf[:, kt*TC:(kt+1)*TC], scr1[:, 0:TC],
                                       p7[:, 0:TC], ALU.mult))
            tgt_ready = ch.n
            ph.barrier()

            wq_base = [wq[0].n, wq[1].n]
            mw_base, mh_base, eh_base = mw.n, mh.n, eh.n
            for hb in range(HB):
                slot = hb % 2
                if hb >= 2:
                    g.wait_ge(mh.sem, mh_base + hb - 1)
                wq[slot].start(
                    w1buf[:, slot*NKT*128:(slot+1)*NKT*128].rearrange(
                        "p (kt o) -> p kt o", kt=NKT),
                    w1r[:, :, hb*128:(hb+1)*128])
            for hb in range(HB):
                slot = hb % 2
                t.wait_ge(ch.sem, tgt_ready)
                t.wait_ge(wq[slot].sem, wq_base[slot] + 16*(hb//2 + 1))
                if hb >= 2:
                    t.wait_ge(eh.sem, eh_base + hb - 1)
                for kt in range(NKT):
                    i = t.matmul(ps_h[slot][:, 0:TC],
                                 w1buf[:, slot*NKT*128 + kt*128:
                                       slot*NKT*128 + (kt+1)*128],
                                 tgt_bf[:, kt*TC:(kt+1)*TC],
                                 start=(kt == 0), stop=(kt == NKT-1))
                mh.inc(i)
            for hb in range(HB):
                slot = hb % 2
                s.wait_ge(mh.sem, mh_base + hb + 1)
                eh.inc(s.activation(out=h_bf[:, hb*TC:(hb+1)*TC],
                                    in_=ps_h[slot][:, 0:TC], func=AF.Relu,
                                    bias=b1_sb[:, hb:hb+1]))

            wq_base = [wq[0].n, wq[1].n]
            mw_base, mh_base, eh_base = mw.n, mh.n, eh.n
            for obg in range(2):
                for kt2 in range(HB):
                    idx = obg*HB + kt2
                    slot = idx % 2
                    if idx >= 2:
                        g.wait_ge(mw.sem, mw_base + idx - 1)
                    wq[slot].start(w2buf[:, slot*512:(slot+1)*512],
                                   w2[kt2*128:(kt2+1)*128, obg*512:(obg+1)*512])
            t.wait_ge(eh.sem, eh_base)
            for obg in range(2):
                if obg == 1:
                    t.wait_ge(eh.sem, eh_base + 1)
                for kt2 in range(HB):
                    idx = obg*HB + kt2
                    slot = idx % 2
                    t.wait_ge(wq[slot].sem, wq_base[slot] + 16*(idx//2 + 1))
                    for ob in range(4):
                        i = t.matmul(ps_eo[ob][:, 0:TC],
                                     w2buf[:, slot*512 + ob*128: slot*512 + (ob+1)*128],
                                     h_bf[:, kt2*TC:(kt2+1)*TC],
                                     start=(kt2 == 0), stop=(kt2 == HB-1))
                    mw.inc(i)
            for obg in range(2):
                s.wait_ge(mw.sem, mw_base + (obg + 1) * HB)
                for ob in range(4):
                    ko = obg*4 + ob
                    i = s.activation(out=eo_sb[:, ko*TC:(ko+1)*TC],
                                     in_=ps_eo[ob][:, 0:TC], func=AF.Identity,
                                     bias=b2_sb[:, ko:ko+1])
                eh.inc(i)
            ch.inc(s.nop())
            eo_ready_eh = eh.n
            ph.barrier()

            bcast_rows(ph, ch, [(mu1, p6), (rs1, p7)], TC, ones_row)
            ch.wait(v)
            v.wait_ge(eh.sem, eo_ready_eh)
            v.wait_ge(st.sem, st.n)
            for kt in range(NKT):
                ch.wait(v)
                ch.inc(v.tensor_tensor(scr1[:, 0:TC], x3kt(kt), p6[:, 0:TC],
                                       ALU.subtract))
                ch.wait(v)
                ch.inc(v.tensor_tensor(scr2[:, 0:TC], scr1[:, 0:TC], p7[:, 0:TC],
                                       ALU.mult))
                ch.wait(v)
                ch.inc(v.tensor_tensor(xi_sb[:, kt*TC:(kt+1)*TC], scr2[:, 0:TC],
                                       eo_sb[:, kt*TC:(kt+1)*TC], ALU.add))

            ch.wait(t); ch.wait(s)
            ln_stats(ph, ch, [xi_sb[:, kt*TC:(kt+1)*TC] for kt in range(NKT)],
                     TC, p6, p7, scr1, ones_col, mu2, rs2, scrow, eps_ap=eps_ap)
            bcast_rows(ph, ch, [(mu2, p6), (rs2, p7)], TC, ones_row)
            ch.wait(v)
            act_done = {}
            for kt in range(NKT):
                scr_o = scr2 if kt % 2 == 0 else scr3
                if kt >= 2:
                    v.wait_ge(ch.sem, act_done[kt - 2])
                ch.wait(v)
                ch.inc(v.tensor_tensor(scr1[:, 0:TC], xi_sb[:, kt*TC:(kt+1)*TC],
                                       p6[:, 0:TC], ALU.subtract))
                ch.wait(v)
                ch.inc(v.tensor_tensor(scr_o[:, 0:TC], scr1[:, 0:TC],
                                       p7[:, 0:TC], ALU.mult))
                s.wait_ge(ch.sem, ch.n)
                ch.inc(s.activation(out=eo_sb[:, kt*TC:(kt+1)*TC],
                                    in_=scr_o[:, 0:TC],
                                    func=AF.Identity,
                                    scale=gm_sb[:, kt:kt+1],
                                    bias=bt_sb[:, kt:kt+1]))
                act_done[kt] = ch.n
            ff_ready = ch.n

            ch.wait(t); ch.wait(s)
            ln_stats(ph, ch, [eo_sb[:, kt*TC:(kt+1)*TC] for kt in range(NKT)],
                     TC, p6, p7, scr1, ones_col, mu3, rs3, scrow, eps_ap=eps_ap)
            bcast_rows(ph, ch, [(mu3, p6), (rs3, p7)], TC, ones_row)
            ch.wait(v)
            for kt in range(NKT):
                ch.wait(v)
                ch.inc(v.tensor_tensor(scr1[:, 0:TC], eo_sb[:, kt*TC:(kt+1)*TC],
                                       p6[:, 0:TC], ALU.subtract))
                ch.wait(v)
                ch.inc(v.tensor_tensor(scr2[:, 0:TC], scr1[:, 0:TC], p7[:, 0:TC],
                                       ALU.mult))
                ch.wait(v)
                ch.inc(v.tensor_tensor(xi_sb[:, kt*TC:(kt+1)*TC], scr2[:, 0:TC],
                                       x3kt(kt), ALU.add))
            ch.wait(sy)
            st.start(x4.rearrange("(kt p) t -> p kt t", p=128)[:, :, t0:t0+TC],
                     xi_sb[:, :].rearrange("p (kt t) -> p kt t", kt=NKT))
            ph.barrier()

        st.wait_all(sy)
        ph.close()
    return nc



def emit_rwkv_a(nc, T, TC=512):
    assert T % TC == 0
    NKT = 8
    NTC = T // TC
    ph = Phase(nc)
    x2 = nc.dram_tensor("x2T", [1024, T + 1], F32, kind="ExternalInput").ap()
    wk = nc.dram_tensor("wkT", [1024, 1024], F32, kind="ExternalInput").ap()
    wv = nc.dram_tensor("wvT", [1024, 1024], F32, kind="ExternalInput").ap()
    wr = nc.dram_tensor("wrT", [1024, 1024], F32, kind="ExternalInput").ap()
    tmd = nc.dram_tensor("tmd", [128, 8], F32, kind="ExternalInput").ap()
    tdd = nc.dram_tensor("tdd", [128, 8], F32, kind="ExternalInput").ap()
    onesd = nc.dram_tensor("onesd", [128, 128], F32, kind="ExternalInput").ap()
    constsd = nc.dram_tensor("constsd", [1, 8], F32, kind="ExternalInput").ap()
    o_k = nc.dram_tensor("o_kT", [1024, T], F32, kind="ExternalOutput").ap()
    o_kv = nc.dram_tensor("o_kvT", [1024, T], F32, kind="ExternalOutput").ap()
    o_sr = nc.dram_tensor("o_srT", [1024, T], F32, kind="ExternalOutput").ap()
    o_sk = nc.dram_tensor("o_SkT", [1024, T], F32, kind="ExternalOutput").ap()
    o_skv = nc.dram_tensor("o_SkvT", [1024, T], F32, kind="ExternalOutput").ap()
    o_m = nc.dram_tensor("o_M", [128, 16], F32, kind="ExternalOutput").ap()

    import contextlib
    with contextlib.ExitStack() as es:
        E = es.enter_context
        x2_sb = E(nc.sbuf_tensor("x2_sb", [128, NKT * TC], F32))
        xh_sb = E(nc.sbuf_tensor("xh_sb", [128, NKT], F32))
        xrs_sb = E(nc.sbuf_tensor("xrs_sb", [128, NKT * (TC + 1)], F32))
        xm_bf = E(nc.sbuf_tensor("xm_bf", [128, NKT * TC], BF16))
        wk_sb = E(nc.sbuf_tensor("wk_sb", [128, NKT * 1024], BF16))
        wv_sb = E(nc.sbuf_tensor("wv_sb", [128, NKT * 1024], BF16))
        wr_sb = E(nc.sbuf_tensor("wr_sb", [128, NKT * 1024], BF16))
        sr_sb = E(nc.sbuf_tensor("sr_sb", [128, NKT * TC], F32))
        scr2 = E(nc.sbuf_tensor("scr2", [128, TC], F32))
        ks_sb = E(nc.sbuf_tensor("ks_sb", [128, NKT * (TC + 2)], F32))
        kvs_sb = E(nc.sbuf_tensor("kvs_sb", [128, NKT * (TC + 2)], F32))
        sk_sb = E(nc.sbuf_tensor("sk_sb", [128, NKT * (TC + 2)], F32))
        skv_sb = E(nc.sbuf_tensor("skv_sb", [128, NKT * (TC + 2)], F32))
        par_sb = E(nc.sbuf_tensor("par_sb", [128, 40], F32))
        ones_row = E(nc.sbuf_tensor("ones_row", [1, 128], F32))
        rows = E(nc.sbuf_tensor("rows", [1, 4 * TC], F32))
        scr1 = E(nc.sbuf_tensor("scr1", [128, TC], F32))
        p0 = E(nc.psum_tensor("p0", [128, TC], F32))
        p1 = E(nc.psum_tensor("p1", [128, TC], F32))
        p2 = E(nc.psum_tensor("p2", [128, TC], F32))
        p6 = E(nc.psum_tensor("p6", [128, TC], F32))
        p7 = E(nc.psum_tensor("p7", [128, TC], F32))

        tm_sb = par_sb[:, 0:8]
        tm1_sb = par_sb[:, 8:16]
        td_sb = par_sb[:, 16:24]
        ones_col = par_sb[:, 32:33]
        eps_ap = par_sb[0:1, 33:34]
        hmu = par_sb[0:1, 34:35]
        hrs = par_sb[0:1, 35:36]
        hsc = par_sb[0:1, 36:37]

        mu1, rs1 = rows[0:1, 0*TC:1*TC], rows[0:1, 1*TC:2*TC]
        scrow = rows[0:1, 2*TC:3*TC]

        ld = DQ(ph, "sync", "ld")
        wld = DQ(ph, "gpsimd", "wld")
        st = DQ(ph, "gpsimd", "st")
        ch = Chain(ph, "ch")
        mm = Chain(ph, "mm")
        ev = Chain(ph, "ev")

        t, v, s, g, sy = (ph.eng["tensor"], ph.eng["vector"], ph.eng["scalar"],
                          ph.eng["gpsimd"], ph.eng["sync"])

        for (wsb, wd) in ((wk_sb, wk), (wv_sb, wv), (wr_sb, wr)):
            wld.start(wsb[:, :].rearrange("p (kt o) -> p kt o", kt=NKT),
                      wd.rearrange("(kt p) o -> p kt o", p=128), serialize=False)
        ld.start(tm_sb, tmd[:, :], serialize=False)
        ld.start(td_sb, tdd[:, :], serialize=False)
        ld.start(ones_row[0:1, :], onesd[0:1, :], serialize=False)
        ld.start(ones_col, onesd[:, 0:1], serialize=False,
                 allow_slow_non_contiguous=True)
        ld.start(eps_ap, constsd[0:1, 0:1], serialize=False,
                 allow_slow_non_contiguous=True)
        ld.start(xh_sb[:, :],
                 x2.rearrange("(kt p) t -> p kt t", p=128)[:, :, 0:1],
                 serialize=False, allow_slow_non_contiguous=True)
        for e in ph.eng.values():
            ld.wait_all(e)
            wld.wait_all(e)
        ch.inc(v.tensor_scalar(tm1_sb, tm_sb, -1.0, 1.0, ALU.mult, ALU.add))
        ch.wait(v)
        ch.inc(v.memset(ks_sb[:, :].rearrange("p (kt t) -> p kt t", kt=NKT)[:, :, 0:1], 0.0))
        ch.wait(v)
        ch.inc(v.memset(kvs_sb[:, :].rearrange("p (kt t) -> p kt t", kt=NKT)[:, :, 0:1], 0.0))
        ch.wait(v)
        ch.inc(v.memset(sk_sb[:, :].rearrange("p (kt t) -> p kt t", kt=NKT)[:, :, 0:1], 0.0))
        ch.wait(v)
        ch.inc(v.memset(skv_sb[:, :].rearrange("p (kt t) -> p kt t", kt=NKT)[:, :, 0:1], 0.0))
        ch.wait(t); ch.wait(s)
        ln_stats(ph, ch, [xh_sb[:, kt:kt+1] for kt in range(NKT)], 1,
                 p6, p7, scr1, ones_col, hmu, hrs, hsc, eps_ap=eps_ap)
        bcast_rows(ph, ch, [(hmu, p6), (hrs, p7)], 1, ones_row)
        ch.wait(v)
        for kt in range(NKT):
            ch.wait(v)
            ch.inc(v.tensor_tensor(scr1[:, 0:1], xh_sb[:, kt:kt+1],
                                   p6[:, 0:1], ALU.subtract))
            ch.wait(v)
            ch.inc(v.tensor_tensor(xrs_sb[:, kt*(TC+1):kt*(TC+1)+1],
                                   scr1[:, 0:1], p7[:, 0:1], ALU.mult))
        ph.barrier()

        dec_bc = par_sb[:, 24:32]
        ch.wait(s)
        ch.inc(s.activation(out=dec_bc, in_=td_sb, func=AF.Exp))
        ch.wait(s)
        ch.inc(s.activation(out=dec_bc, in_=dec_bc, func=AF.Exp, scale=-1.0))
        ph.barrier()

        for c in range(NTC):
            t0 = c * TC
            W = TC + 1 if c == NTC - 1 else TC
            sy.wait_ge(ch.sem, ch.n)
            ld.start(x2_sb[:, :].rearrange("p (kt t) -> p kt t", kt=NKT),
                     x2.rearrange("(kt p) t -> p kt t", p=128)[
                         :, :, 1 + t0: 1 + t0 + TC])
            for e in (t, v, s):
                ld.wait_all(e)
                st.wait_all(e)
            x2kt = lambda kt: x2_sb[:, kt*TC:(kt+1)*TC]

            ch.wait(t); ch.wait(s)
            ln_stats(ph, ch, [x2kt(kt) for kt in range(NKT)], TC, p6, p7,
                     scr1, ones_col, mu1, rs1, scrow, eps_ap=eps_ap)
            bcast_rows(ph, ch, [(mu1, p6), (rs1, p7)], TC, ones_row)
            ch.wait(v)
            for kt in range(NKT):
                ch.wait(v)
                ch.inc(v.tensor_tensor(scr1[:, 0:TC], x2kt(kt), p6[:, 0:TC],
                                       ALU.subtract))
                ch.wait(v)
                ch.inc(v.tensor_tensor(xrs_sb[:, kt*(TC+1)+1: kt*(TC+1)+1+TC],
                                       scr1[:, 0:TC], p7[:, 0:TC], ALU.mult))
            for kt in range(NKT):
                ch.wait(v)
                ch.inc(v.tensor_scalar(scr1[:, 0:TC],
                                       xrs_sb[:, kt*(TC+1)+1: kt*(TC+1)+1+TC],
                                       tm_sb[:, kt:kt+1], None, ALU.mult))
                ch.wait(v)
                ch.inc(v.scalar_tensor_tensor(xm_bf[:, kt*TC:(kt+1)*TC],
                                              xrs_sb[:, kt*(TC+1): kt*(TC+1)+TC],
                                              tm1_sb[:, kt:kt+1], scr1[:, 0:TC],
                                              ALU.mult, ALU.add))
            ch.wait(v)
            ch.inc(v.tensor_copy(
                xrs_sb[:, :].rearrange("p (kt t) -> p kt t", kt=NKT)[:, :, 0:1],
                xrs_sb[:, :].rearrange("p (kt t) -> p kt t", kt=NKT)[:, :, TC:TC+1]))
            xm_ready = ch.n
            ph.barrier()

            for ob in range(NKT):
                t.wait_ge(ch.sem, xm_ready)
                if c > 0 or ob > 0:
                    t.wait_ge(ev.sem, ev.n)
                for kt in range(NKT):
                    t.matmul(p0[:, 0:TC], wk_sb[:, kt*1024 + ob*128: kt*1024 + (ob+1)*128],
                             xm_bf[:, kt*TC:(kt+1)*TC], start=(kt == 0),
                             stop=(kt == NKT-1))
                for kt in range(NKT):
                    t.matmul(p1[:, 0:TC], wv_sb[:, kt*1024 + ob*128: kt*1024 + (ob+1)*128],
                             xm_bf[:, kt*TC:(kt+1)*TC], start=(kt == 0),
                             stop=(kt == NKT-1))
                for kt in range(NKT):
                    i = t.matmul(p2[:, 0:TC], wr_sb[:, kt*1024 + ob*128: kt*1024 + (ob+1)*128],
                                 xm_bf[:, kt*TC:(kt+1)*TC], start=(kt == 0),
                                 stop=(kt == NKT-1))
                mm.inc(i)
                v.wait_ge(mm.sem, mm.n)
                ch.wait(v)
                ev.inc(v.tensor_scalar(scr1[:, 0:TC], p0[:, 0:TC],
                                       60.0, None, ALU.min))
                ch.inc(ph.eng["vector"].nop()) if False else None
                ch.wait(s)
                s.wait_ge(ev.sem, ev.n)
                ch.inc(s.activation(out=ks_sb[:, ob*(TC+2)+1: ob*(TC+2)+1+TC],
                                    in_=scr1[:, 0:TC], func=AF.Exp))
                s.wait_ge(mm.sem, mm.n)
                ch.wait(s)
                ch.inc(s.activation(out=scr2[:, 0:TC], in_=p1[:, 0:TC],
                                    func=AF.Identity))
                ev.inc(s.activation(out=sr_sb[:, ob*TC:(ob+1)*TC], in_=p2[:, 0:TC],
                                    func=AF.Sigmoid))
                ch.wait(v)
                v.wait_ge(ev.sem, ev.n)
                ch.inc(v.tensor_tensor(kvs_sb[:, ob*(TC+2)+1: ob*(TC+2)+1+TC],
                                       ks_sb[:, ob*(TC+2)+1: ob*(TC+2)+1+TC],
                                       scr2[:, 0:TC], ALU.mult))
                ev.inc(v.nop())
            for kt in range(NKT):
                d0 = dec_bc[:, kt:kt+1].broadcast_to([128, W])
                ch.wait(v)
                ch.inc(v.tensor_tensor_scan(
                    sk_sb[:, kt*(TC+2)+1: kt*(TC+2)+1+W],
                    d0, ks_sb[:, kt*(TC+2): kt*(TC+2)+W],
                    sk_sb[:, kt*(TC+2): kt*(TC+2)+1],
                    ALU.mult, ALU.add))
                ch.wait(v)
                ch.inc(v.tensor_tensor_scan(
                    skv_sb[:, kt*(TC+2)+1: kt*(TC+2)+1+W],
                    d0, kvs_sb[:, kt*(TC+2): kt*(TC+2)+W],
                    skv_sb[:, kt*(TC+2): kt*(TC+2)+1],
                    ALU.mult, ALU.add))
            scans_done = ch.n
            ch.wait(g)
            st.start(o_k.rearrange("(kt p) t -> p kt t", p=128)[:, :, t0:t0+TC],
                     ks_sb[:, :].rearrange("p (kt t) -> p kt t", kt=NKT)[:, :, 1:TC+1])
            st.start(o_kv.rearrange("(kt p) t -> p kt t", p=128)[:, :, t0:t0+TC],
                     kvs_sb[:, :].rearrange("p (kt t) -> p kt t", kt=NKT)[:, :, 1:TC+1])
            st.start(o_sr.rearrange("(kt p) t -> p kt t", p=128)[:, :, t0:t0+TC],
                     sr_sb[:, :].rearrange("p (kt t) -> p kt t", kt=NKT))
            st.start(o_sk.rearrange("(kt p) t -> p kt t", p=128)[:, :, t0:t0+TC],
                     sk_sb[:, :].rearrange("p (kt t) -> p kt t", kt=NKT)[:, :, 1:TC+1])
            st.start(o_skv.rearrange("(kt p) t -> p kt t", p=128)[:, :, t0:t0+TC],
                     skv_sb[:, :].rearrange("p (kt t) -> p kt t", kt=NKT)[:, :, 1:TC+1])
            if c == NTC - 1:
                st.start(o_m[:, 0:8],
                         sk_sb[:, :].rearrange("p (kt t) -> p kt t", kt=NKT)[:, :, TC+1:TC+2],
                         allow_slow_non_contiguous=True)
                st.start(o_m[:, 8:16],
                         skv_sb[:, :].rearrange("p (kt t) -> p kt t", kt=NKT)[:, :, TC+1:TC+2],
                         allow_slow_non_contiguous=True)
            else:
                ch.wait(v)
                ch.inc(v.tensor_copy(
                    sk_sb[:, :].rearrange("p (kt t) -> p kt t", kt=NKT)[:, :, 0:1],
                    sk_sb[:, :].rearrange("p (kt t) -> p kt t", kt=NKT)[:, :, TC:TC+1]))
                ch.wait(v)
                ch.inc(v.tensor_copy(
                    skv_sb[:, :].rearrange("p (kt t) -> p kt t", kt=NKT)[:, :, 0:1],
                    skv_sb[:, :].rearrange("p (kt t) -> p kt t", kt=NKT)[:, :, TC:TC+1]))
                ch.wait(v)
                ch.inc(v.tensor_copy(
                    ks_sb[:, :].rearrange("p (kt t) -> p kt t", kt=NKT)[:, :, 0:1],
                    ks_sb[:, :].rearrange("p (kt t) -> p kt t", kt=NKT)[:, :, TC:TC+1]))
                ch.wait(v)
                ch.inc(v.tensor_copy(
                    kvs_sb[:, :].rearrange("p (kt t) -> p kt t", kt=NKT)[:, :, 0:1],
                    kvs_sb[:, :].rearrange("p (kt t) -> p kt t", kt=NKT)[:, :, TC:TC+1]))
            ph.barrier()

        st.wait_all(g)
        for e in ph.eng.values():
            st.wait_all(e)
        ph.close()
    return nc



def emit_rwkv_b(nc, T, TC=512):
    assert T % TC == 0
    NKT = 8
    NTC = T // TC
    ph = Phase(nc)
    i_k = nc.dram_tensor("i_kT", [1024, T], F32, kind="ExternalInput").ap()
    i_kv = nc.dram_tensor("i_kvT", [1024, T], F32, kind="ExternalInput").ap()
    i_sr = nc.dram_tensor("i_srT", [1024, T], F32, kind="ExternalInput").ap()
    i_sk = nc.dram_tensor("i_SkT", [1024, T], F32, kind="ExternalInput").ap()
    i_skv = nc.dram_tensor("i_SkvT", [1024, T], F32, kind="ExternalInput").ap()
    i_m = nc.dram_tensor("i_Mall", [128, 64], F32, kind="ExternalInput").ap()
    i_sel = nc.dram_tensor("i_sel", [128, 4], F32, kind="ExternalInput").ap()
    x2 = nc.dram_tensor("x2T", [1024, T], F32, kind="ExternalInput").ap()
    wo = nc.dram_tensor("woT", [1024, 1024], F32, kind="ExternalInput").ap()
    tdd = nc.dram_tensor("tdd", [128, 8], F32, kind="ExternalInput").ap()
    tfd = nc.dram_tensor("tfd", [128, 8], F32, kind="ExternalInput").ap()
    onesd = nc.dram_tensor("onesd", [128, 128], F32, kind="ExternalInput").ap()
    constsd = nc.dram_tensor("constsd", [1, 8], F32, kind="ExternalInput").ap()
    tbase = nc.dram_tensor("tbase", [1, 4], F32, kind="ExternalInput").ap()
    x3 = nc.dram_tensor("x3T", [1024, T], F32, kind="ExternalOutput").ap()

    import contextlib
    with contextlib.ExitStack() as es:
        E = es.enter_context
        k_sb = E(nc.sbuf_tensor("k_sb", [128, NKT * TC], F32))
        kv_sb = E(nc.sbuf_tensor("kv_sb", [128, NKT * TC], F32))
        sr_sb = E(nc.sbuf_tensor("sr_sb", [128, NKT * TC], F32))
        sk_sb = E(nc.sbuf_tensor("sk_sb", [128, NKT * TC], F32))
        skv_sb = E(nc.sbuf_tensor("skv_sb", [128, NKT * TC], F32))
        x2_sb = E(nc.sbuf_tensor("x2_sb", [128, NKT * TC], F32))
        ro_sb = E(nc.sbuf_tensor("ro_sb", [128, NKT * TC], F32))
        rw_bf = E(nc.sbuf_tensor("rw_bf", [128, NKT * TC], BF16))
        wo_sb = E(nc.sbuf_tensor("wo_sb", [128, NKT * 1024], BF16))
        m_sb = E(nc.sbuf_tensor("m_sb", [128, 64], F32))
        par_sb = E(nc.sbuf_tensor("par_sb", [128, 64], F32))
        dp_sb = E(nc.sbuf_tensor("dp_sb", [128, TC], F32))
        io_sb = E(nc.sbuf_tensor("io_sb", [128, TC], F32))
        scr1 = E(nc.sbuf_tensor("scr1", [128, TC], F32))
        scr2 = E(nc.sbuf_tensor("scr2", [128, TC], F32))
        ones_row = E(nc.sbuf_tensor("ones_row", [1, 128], F32))
        rows = E(nc.sbuf_tensor("rows", [1, 4 * TC], F32))
        p0 = E(nc.psum_tensor("p0", [128, TC], F32))
        p1 = E(nc.psum_tensor("p1", [128, TC], F32))
        p6 = E(nc.psum_tensor("p6", [128, TC], F32))
        p7 = E(nc.psum_tensor("p7", [128, TC], F32))

        td_sb = par_sb[:, 0:8]
        tf_sb = par_sb[:, 8:16]
        lnd_sb = par_sb[:, 16:24]
        dfull_sb = par_sb[:, 24:32]
        tfx_sb = par_sb[:, 32:40]
        pk_sb = par_sb[:, 40:48]
        pkv_sb = par_sb[:, 48:56]
        sel_sb = par_sb[:, 56:60]
        ones_col = par_sb[:, 60:61]
        eps_ap = par_sb[0:1, 61:62]
        t1_sb = m_sb[:, 16*4:16*4]

        mu1, rs1 = rows[0:1, 0:TC], rows[0:1, TC:2*TC]
        scrow = rows[0:1, 2*TC:3*TC]

        ld = DQ(ph, "sync", "ld")
        wld = DQ(ph, "gpsimd", "wld")
        st = DQ(ph, "gpsimd", "st")
        ch = Chain(ph, "ch")
        mm = Chain(ph, "mm")
        ev = Chain(ph, "ev")

        t, v, s, g, sy = (ph.eng["tensor"], ph.eng["vector"], ph.eng["scalar"],
                          ph.eng["gpsimd"], ph.eng["sync"])

        wld.start(wo_sb[:, :].rearrange("p (kt o) -> p kt o", kt=NKT),
                  wo.rearrange("(kt p) o -> p kt o", p=128), serialize=False)
        ld.start(td_sb, tdd[:, :], serialize=False)
        ld.start(tf_sb, tfd[:, :], serialize=False)
        ld.start(m_sb[:, :], i_m[:, :], serialize=False)
        ld.start(sel_sb, i_sel[:, :], serialize=False)
        ld.start(ones_row[0:1, :], onesd[0:1, :], serialize=False)
        ld.start(ones_col, onesd[:, 0:1], serialize=False,
                 allow_slow_non_contiguous=True)
        ld.start(eps_ap, constsd[0:1, 0:1], serialize=False,
                 allow_slow_non_contiguous=True)
        for e in ph.eng.values():
            ld.wait_all(e)
            wld.wait_all(e)
        ch.wait(s)
        ch.inc(s.activation(out=lnd_sb, in_=td_sb, func=AF.Exp))
        ch.wait(s)
        ch.inc(s.activation(out=dfull_sb, in_=lnd_sb, func=AF.Exp,
                            scale=-float(T)))
        ch.wait(v)
        ch.inc(v.tensor_scalar(lnd_sb, lnd_sb, -1.0, None, ALU.mult))
        ch.wait(s)
        ch.inc(s.activation(out=tfx_sb, in_=tf_sb, func=AF.Exp))
        ch.wait(v)
        ch.inc(v.memset(pk_sb, 0.0))
        ch.wait(v)
        ch.inc(v.memset(pkv_sb, 0.0))
        for j in range(4):
            for (P, moff) in ((pk_sb, 16*j), (pkv_sb, 16*j+8)):
                ch.wait(v)
                ch.inc(v.tensor_tensor(scr1[:, 0:8], P, dfull_sb, ALU.mult))
                ch.wait(v)
                ch.inc(v.tensor_tensor(scr1[:, 0:8], scr1[:, 0:8],
                                       m_sb[:, moff:moff+8], ALU.add))
                ch.wait(v)
                ch.inc(v.tensor_tensor(scr1[:, 8:16], scr1[:, 0:8],
                                       P, ALU.subtract))
                ch.wait(v)
                ch.inc(v.scalar_tensor_tensor(P, scr1[:, 8:16],
                                              sel_sb[:, j:j+1], P,
                                              ALU.mult, ALU.add))
        ph.barrier()

        for c in range(NTC):
            t0 = c * TC
            def seg(buf, kt):
                return buf[:, kt*TC:(kt+1)*TC]
            sy.wait_ge(ch.sem, ch.n)
            sy.wait_ge(mm.sem, mm.n)
            for (sb, dr) in ((k_sb, i_k), (kv_sb, i_kv), (sr_sb, i_sr),
                             (sk_sb, i_sk), (skv_sb, i_skv), (x2_sb, x2)):
                ld.start(sb[:, :].rearrange("p (kt t) -> p kt t", kt=NKT),
                         dr.rearrange("(kt p) t -> p kt t", p=128)[
                             :, :, t0:t0+TC])
            for e in (t, v, s, g):
                ld.wait_all(e)
                st.wait_all(e)
            ch.wait(g)
            ch.inc(g.iota(io_sb[:, 0:TC], pattern=[[1, TC]], base=t0,
                          channel_multiplier=0,
                          allow_small_or_imprecise_dtypes=True))
            for kt in range(NKT):
                ch.wait(s)
                ch.inc(s.activation(out=dp_sb[:, 0:TC], in_=io_sb[:, 0:TC],
                                    func=AF.Exp, scale=lnd_sb[:, kt:kt+1]))
                ch.wait(v)
                ch.inc(v.scalar_tensor_tensor(seg(sk_sb, kt), dp_sb[:, 0:TC],
                                              pk_sb[:, kt:kt+1], seg(sk_sb, kt),
                                              ALU.mult, ALU.add))
                ch.wait(v)
                ch.inc(v.scalar_tensor_tensor(seg(skv_sb, kt), dp_sb[:, 0:TC],
                                              pkv_sb[:, kt:kt+1], seg(skv_sb, kt),
                                              ALU.mult, ALU.add))
                ch.wait(v)
                ch.inc(v.scalar_tensor_tensor(seg(skv_sb, kt), seg(kv_sb, kt),
                                              tfx_sb[:, kt:kt+1], seg(skv_sb, kt),
                                              ALU.mult, ALU.add))
                ch.wait(v)
                ch.inc(v.scalar_tensor_tensor(seg(sk_sb, kt), seg(k_sb, kt),
                                              tfx_sb[:, kt:kt+1], seg(sk_sb, kt),
                                              ALU.mult, ALU.add))
                ch.wait(v)
                ch.inc(v.tensor_scalar(seg(sk_sb, kt), seg(sk_sb, kt), 1e-16,
                                       None, ALU.add))
                ch.wait(v)
                ch.inc(v.reciprocal(seg(sk_sb, kt), seg(sk_sb, kt)))
                ch.wait(v)
                ch.inc(v.tensor_tensor(seg(skv_sb, kt), seg(skv_sb, kt),
                                       seg(sk_sb, kt), ALU.mult))
                ch.wait(v)
                ch.inc(v.tensor_tensor(rw_bf[:, kt*TC:(kt+1)*TC],
                                       seg(skv_sb, kt), seg(sr_sb, kt),
                                       ALU.mult))
            rw_ready = ch.n
            for ob in range(NKT):
                t.wait_ge(ch.sem, rw_ready)
                t.wait_ge(ev.sem, ev.n)
                for kt in range(NKT):
                    i = t.matmul(p0[:, 0:TC],
                                 wo_sb[:, kt*1024 + ob*128: kt*1024 + (ob+1)*128],
                                 rw_bf[:, kt*TC:(kt+1)*TC],
                                 start=(kt == 0), stop=(kt == NKT-1))
                mm.inc(i)
                s.wait_ge(mm.sem, mm.n)
                ev.inc(s.activation(out=seg(ro_sb, ob), in_=p0[:, 0:TC],
                                    func=AF.Identity))
            ch.wait(t); ch.wait(s)
            s.wait_ge(ev.sem, ev.n)
            t.wait_ge(ev.sem, ev.n)
            ln_stats(ph, ch, [seg(ro_sb, kt) for kt in range(NKT)], TC, p6, p7,
                     scr1, ones_col, mu1, rs1, scrow, eps_ap=eps_ap)
            bcast_rows(ph, ch, [(mu1, p6), (rs1, p7)], TC, ones_row)
            ch.wait(v)
            v.wait_ge(st.sem, st.n)
            for kt in range(NKT):
                ch.wait(v)
                ch.inc(v.tensor_tensor(scr1[:, 0:TC], seg(ro_sb, kt),
                                       p6[:, 0:TC], ALU.subtract))
                ch.wait(v)
                ch.inc(v.tensor_tensor(scr2[:, 0:TC], scr1[:, 0:TC],
                                       p7[:, 0:TC], ALU.mult))
                ch.wait(v)
                ch.inc(v.tensor_tensor(seg(ro_sb, kt), scr2[:, 0:TC],
                                       seg(x2_sb, kt), ALU.add))
            ch.wait(g)
            st.start(x3.rearrange("(kt p) t -> p kt t", p=128)[:, :, t0:t0+TC],
                     ro_sb[:, :].rearrange("p (kt t) -> p kt t", kt=NKT))
            ph.barrier()

        for e in ph.eng.values():
            st.wait_all(e)
        ph.close()
    return nc



def emit_mamba_fe(nc, T, TC=512):
    assert T % TC == 0
    NKT = 8
    NOB = 16
    NTC = T // TC
    ph = Phase(nc)
    x = nc.dram_tensor("xT", [1024, T + 3], F32, kind="ExternalInput").ap()
    w_in = nc.dram_tensor("w_in", [1024, 4096], F32, kind="ExternalInput").ap()
    w_xp = nc.dram_tensor("w_xp", [2048, 96], F32, kind="ExternalInput").ap()
    w_dt = nc.dram_tensor("w_dt", [64, 2048], F32, kind="ExternalInput").ap()
    cwd = nc.dram_tensor("cwd", [128, 64], F32, kind="ExternalInput").ap()
    cbd = nc.dram_tensor("cbd", [128, 16], F32, kind="ExternalInput").ap()
    bdtd = nc.dram_tensor("bdtd", [128, 16], F32, kind="ExternalInput").ap()
    onesd = nc.dram_tensor("onesd", [128, 128], F32, kind="ExternalInput").ap()
    constsd = nc.dram_tensor("constsd", [1, 8], F32, kind="ExternalInput").ap()
    o_d = nc.dram_tensor("o_dT", [2048, T], F32, kind="ExternalOutput").ap()
    o_u = nc.dram_tensor("o_uT", [2048, T], F32, kind="ExternalOutput").ap()
    o_z = nc.dram_tensor("o_zT", [2048, T], F32, kind="ExternalOutput").ap()
    o_b = nc.dram_tensor("o_BT", [16, T], F32, kind="ExternalOutput").ap()
    o_c = nc.dram_tensor("o_CT", [16, T], F32, kind="ExternalOutput").ap()

    import contextlib
    with contextlib.ExitStack() as es:
        E = es.enter_context
        x_sb = E(nc.sbuf_tensor("x_sb", [128, NKT * TC], F32))
        xh_sb = E(nc.sbuf_tensor("xh_sb", [128, NKT * 3], F32))
        lnx_bf = E(nc.sbuf_tensor("lnx_bf", [128, NKT * TC], BF16))
        lnxh_bf = E(nc.sbuf_tensor("lnxh_bf", [128, NKT * 3], BF16))
        win_sb = E(nc.sbuf_tensor("win_sb", [128, NKT * 4096], BF16))
        wxp_sb = E(nc.sbuf_tensor("wxp_sb", [128, NOB * 96], BF16))
        wdt_sb = E(nc.sbuf_tensor("wdt_sb", [64, NOB * 128], BF16))
        u_bf = E(nc.sbuf_tensor("u_bf", [128, NOB * TC], BF16))
        xin_sb = E(nc.sbuf_tensor("xin_sb", [128, TC + 3], F32))
        carry_sb = E(nc.sbuf_tensor("carry_sb", [128, NOB * 3], F32))
        stg = E(nc.sbuf_tensor("stg", [128, 2 * TC], F32))
        xdbl_sb = E(nc.sbuf_tensor("xdbl_sb", [96, TC], F32))
        dtl_bf = E(nc.sbuf_tensor("dtl_bf", [64, TC], BF16))
        scr1 = E(nc.sbuf_tensor("scr1", [128, TC], F32))
        scr2 = E(nc.sbuf_tensor("scr2", [128, TC], F32))
        par_sb = E(nc.sbuf_tensor("par_sb", [128, 100], F32))
        ones_row = E(nc.sbuf_tensor("ones_row", [1, 128], F32))
        rows = E(nc.sbuf_tensor("rows", [1, 4 * TC], F32))
        p0 = E(nc.psum_tensor("p0", [128, TC], F32))
        p1 = E(nc.psum_tensor("p1", [128, TC], F32))
        p6 = E(nc.psum_tensor("p6", [128, TC], F32))
        p7 = E(nc.psum_tensor("p7", [128, TC], F32))

        cw_sb = par_sb[:, 0:64]
        cb_sb = par_sb[:, 64:80]
        bdt_sb = par_sb[:, 80:96]
        ones_col = par_sb[:, 96:97]
        eps_ap = par_sb[0:1, 97:98]
        mu1, rs1 = rows[0:1, 0:TC], rows[0:1, TC:2*TC]
        scrow = rows[0:1, 2*TC:3*TC]
        hmu, hrs, hsc = (rows[0:1, 3*TC:3*TC+3], rows[0:1, 3*TC+3:3*TC+6],
                         rows[0:1, 3*TC+6:3*TC+9])

        ld = DQ(ph, "sync", "ld")
        wld = DQ(ph, "gpsimd", "wld")
        st = DQ(ph, "gpsimd", "st")
        st2 = DQ(ph, "sync", "st2")
        ch = Chain(ph, "ch")
        mm = Chain(ph, "mm")
        ev = Chain(ph, "ev")

        t, v, s, g, sy = (ph.eng["tensor"], ph.eng["vector"], ph.eng["scalar"],
                          ph.eng["gpsimd"], ph.eng["sync"])

        wld.start(win_sb[:, :].rearrange("p (kt o) -> p kt o", kt=NKT),
                  w_in.rearrange("(kt p) o -> p kt o", p=128), serialize=False)
        wld.start(wxp_sb[:, :].rearrange("p (kt o) -> p kt o", kt=NOB),
                  w_xp.rearrange("(kt p) o -> p kt o", p=128), serialize=False)
        wld.start(wdt_sb[:, :].rearrange("p (kt o) -> p kt o", kt=NOB),
                  w_dt.rearrange("kk (kt o) -> kk kt o", kt=NOB), serialize=False)
        ld.start(cw_sb, cwd[:, :], serialize=False)
        ld.start(cb_sb, cbd[:, :], serialize=False)
        ld.start(bdt_sb, bdtd[:, :], serialize=False)
        ld.start(ones_row[0:1, :], onesd[0:1, :], serialize=False)
        ld.start(ones_col, onesd[:, 0:1], serialize=False,
                 allow_slow_non_contiguous=True)
        ld.start(eps_ap, constsd[0:1, 0:1], serialize=False,
                 allow_slow_non_contiguous=True)
        ld.start(xh_sb[:, :].rearrange("p (kt t) -> p kt t", kt=NKT),
                 x.rearrange("(kt p) t -> p kt t", p=128)[:, :, 0:3],
                 serialize=False)
        for e in ph.eng.values():
            ld.wait_all(e)
            wld.wait_all(e)

        ch.wait(t); ch.wait(s)
        ln_stats(ph, ch, [xh_sb[:, kt*3:(kt+1)*3] for kt in range(NKT)], 3,
                 p6, p7, scr1, ones_col, hmu, hrs, hsc, eps_ap=eps_ap)
        bcast_rows(ph, ch, [(hmu, p6), (hrs, p7)], 3, ones_row)
        ch.wait(v)
        for kt in range(NKT):
            ch.wait(v)
            ch.inc(v.tensor_tensor(scr1[:, 0:3], xh_sb[:, kt*3:(kt+1)*3],
                                   p6[:, 0:3], ALU.subtract))
            ch.wait(v)
            ch.inc(v.tensor_tensor(lnxh_bf[:, kt*3:(kt+1)*3], scr1[:, 0:3],
                                   p7[:, 0:3], ALU.mult))
        lnxh_ready = ch.n
        for ob in range(NOB):
            t.wait_ge(ch.sem, lnxh_ready)
            if ob > 0:
                t.wait_ge(ev.sem, ev.n)
            for kt in range(NKT):
                i = t.matmul(p0[:, 0:3],
                             win_sb[:, kt*4096 + ob*128: kt*4096 + (ob+1)*128],
                             lnxh_bf[:, kt*3:(kt+1)*3],
                             start=(kt == 0), stop=(kt == NKT-1))
            mm.inc(i)
            s.wait_ge(mm.sem, mm.n)
            ev.inc(s.activation(out=carry_sb[:, ob*3:(ob+1)*3], in_=p0[:, 0:3],
                                func=AF.Identity))
        ph.barrier()

        for c in range(NTC):
            t0 = c * TC
            sy.wait_ge(ch.sem, ch.n)
            sy.wait_ge(mm.sem, mm.n)
            ld.start(x_sb[:, :].rearrange("p (kt t) -> p kt t", kt=NKT),
                     x.rearrange("(kt p) t -> p kt t", p=128)[
                         :, :, 3 + t0: 3 + t0 + TC])
            for e in (t, v, s):
                ld.wait_all(e)
                st.wait_all(e)
                st2.wait_all(e)
            xkt = lambda kt: x_sb[:, kt*TC:(kt+1)*TC]

            ch.wait(t); ch.wait(s)
            ln_stats(ph, ch, [xkt(kt) for kt in range(NKT)], TC, p6, p7,
                     scr1, ones_col, mu1, rs1, scrow, eps_ap=eps_ap)
            bcast_rows(ph, ch, [(mu1, p6), (rs1, p7)], TC, ones_row)
            ch.wait(v)
            for kt in range(NKT):
                ch.wait(v)
                ch.inc(v.tensor_tensor(scr1[:, 0:TC], xkt(kt), p6[:, 0:TC],
                                       ALU.subtract))
                ch.wait(v)
                ch.inc(v.tensor_tensor(lnx_bf[:, kt*TC:(kt+1)*TC], scr1[:, 0:TC],
                                       p7[:, 0:TC], ALU.mult))
            lnx_ready = ch.n
            ph.barrier()

            for ob in range(NOB):
                t.wait_ge(ch.sem, lnx_ready)
                t.wait_ge(ev.sem, ev.n)
                for kt in range(NKT):
                    i = t.matmul(p0[:, 0:TC],
                                 win_sb[:, kt*4096 + ob*128: kt*4096 + (ob+1)*128],
                                 lnx_bf[:, kt*TC:(kt+1)*TC],
                                 start=(kt == 0), stop=(kt == NKT-1))
                mm.inc(i)
                s.wait_ge(mm.sem, mm.n)
                ch.wait(s)
                ch.inc(s.activation(out=xin_sb[:, 3:3+TC], in_=p0[:, 0:TC],
                                    func=AF.Identity))
                ch.wait(v)
                ch.inc(v.tensor_copy(xin_sb[:, 0:3],
                                     carry_sb[:, ob*3:(ob+1)*3]))
                ch.wait(v)
                ch.inc(v.tensor_scalar(scr1[:, 0:TC], xin_sb[:, 0:TC],
                                       cw_sb[:, 4*ob:4*ob+1], None, ALU.mult))
                for kk in range(1, 4):
                    ch.wait(v)
                    ch.inc(v.scalar_tensor_tensor(scr1[:, 0:TC],
                                                  xin_sb[:, kk:kk+TC],
                                                  cw_sb[:, 4*ob+kk:4*ob+kk+1],
                                                  scr1[:, 0:TC],
                                                  ALU.mult, ALU.add))
                ch.wait(v)
                ch.inc(v.tensor_copy(carry_sb[:, ob*3:(ob+1)*3],
                                     xin_sb[:, TC:TC+3]))
                s.wait_ge(st.sem, st.n)
                ch.wait(s)
                ch.inc(s.activation(out=stg[:, 0:TC], in_=scr1[:, 0:TC],
                                    func=AF.Silu, bias=cb_sb[:, ob:ob+1]))
                ch.wait(v)
                ch.inc(v.tensor_copy(u_bf[:, ob*TC:(ob+1)*TC], stg[:, 0:TC]))
                ch.wait(g)
                st.start(o_u[ob*128:(ob+1)*128, t0:t0+TC], stg[:, 0:TC])
                ev.inc(g.engine_nop())
            ph.barrier()

            for ob in range(NOB):
                t.wait_ge(ev.sem, ev.n)
                for kt in range(NKT):
                    i = t.matmul(p0[:, 0:TC],
                                 win_sb[:, kt*4096 + (NOB+ob)*128:
                                        kt*4096 + (NOB+ob+1)*128],
                                 lnx_bf[:, kt*TC:(kt+1)*TC],
                                 start=(kt == 0), stop=(kt == NKT-1))
                mm.inc(i)
                s.wait_ge(mm.sem, mm.n)
                s.wait_ge(st2.sem, st2.n)
                ch.wait(s)
                ch.inc(s.activation(out=stg[:, TC:2*TC], in_=p0[:, 0:TC],
                                    func=AF.Identity))
                ch.wait(sy)
                st2.start(o_z[ob*128:(ob+1)*128, t0:t0+TC], stg[:, TC:2*TC])
                ev.inc(sy.nop())
            ph.barrier()

            t.wait_ge(ev.sem, ev.n)
            for kt in range(NOB):
                i = t.matmul(p1[0:96, 0:TC], wxp_sb[:, kt*96:(kt+1)*96],
                             u_bf[:, kt*TC:(kt+1)*TC],
                             start=(kt == 0), stop=(kt == NOB-1))
            mm.inc(i)
            s.wait_ge(mm.sem, mm.n)
            ch.wait(s)
            ch.inc(s.activation(out=xdbl_sb[:, 0:TC], in_=p1[0:96, 0:TC],
                                func=AF.Identity))
            ch.wait(g)
            st.start(o_b[:, t0:t0+TC], xdbl_sb[64:80, 0:TC])
            ch.wait(g)
            st.start(o_c[:, t0:t0+TC], xdbl_sb[80:96, 0:TC])
            ch.wait(v)
            ch.inc(v.tensor_copy(dtl_bf[:, 0:TC], xdbl_sb[0:64, 0:TC]))
            dtl_ready = ch.n
            ph.barrier()

            relu_done = None
            for ob in range(NOB):
                t.wait_ge(ch.sem, dtl_ready)
                if relu_done is not None:
                    t.wait_ge(ch.sem, relu_done)
                i = t.matmul(p0[:, 0:TC], wdt_sb[:, ob*128:(ob+1)*128],
                             dtl_bf[:, 0:TC], start=True, stop=True)
                mm.inc(i)
                s.wait_ge(mm.sem, mm.n)
                s.wait_ge(st.sem, st.n)
                ch.wait(s)
                ch.inc(s.activation(out=scr1[:, 0:TC], in_=p0[:, 0:TC],
                                    func=AF.Abs, bias=bdt_sb[:, ob:ob+1]))
                ch.wait(s)
                ch.inc(s.activation(out=scr1[:, 0:TC], in_=scr1[:, 0:TC],
                                    func=AF.Exp, scale=-1.0))
                ch.wait(s)
                ch.inc(s.activation(out=scr1[:, 0:TC], in_=scr1[:, 0:TC],
                                    func=AF.Ln, bias=ones_col[:, 0:1]))
                ch.wait(s)
                ch.inc(s.activation(out=scr2[:, 0:TC], in_=p0[:, 0:TC],
                                    func=AF.Relu, bias=bdt_sb[:, ob:ob+1]))
                relu_done = ch.n
                ch.wait(v)
                v.wait_ge(st.sem, st.n)
                ch.inc(v.tensor_tensor(scr2[:, 0:TC], scr1[:, 0:TC],
                                       scr2[:, 0:TC], ALU.add))
                ch.wait(g)
                st.start(o_d[ob*128:(ob+1)*128, t0:t0+TC], scr2[:, 0:TC])
            ph.barrier()

        for e in ph.eng.values():
            st.wait_all(e)
            st2.wait_all(e)
        ph.close()
    return nc



def emit_mamba_scan(nc, T, TC=512):
    assert T % TC == 0
    NTC = T // TC
    ph = Phase(nc)
    i_d = nc.dram_tensor("i_dT", [256, 2, T], F32, kind="ExternalInput").ap()
    i_u = nc.dram_tensor("i_uT", [256, 2, T], F32, kind="ExternalInput").ap()
    i_z = nc.dram_tensor("i_zT", [256, 2, T], F32, kind="ExternalInput").ap()
    i_b = nc.dram_tensor("i_BT", [2, 16, T], F32, kind="ExternalInput").ap()
    i_c = nc.dram_tensor("i_CT", [2, 16, T], F32, kind="ExternalInput").ap()
    i_al = nc.dram_tensor("i_Alog", [4096], F32, kind="ExternalInput").ap()
    i_e = nc.dram_tensor("i_E", [128, 16 * 128], F32, kind="ExternalInput").ap()
    i_r = nc.dram_tensor("i_R", [128, 16 * 128], F32, kind="ExternalInput").ap()
    i_bc = nc.dram_tensor("i_RepBC", [16, 128], F32, kind="ExternalInput").ap()
    i_ds = nc.dram_tensor("i_Dskip", [128, 2], F32, kind="ExternalInput").ap()
    i_wo = nc.dram_tensor("i_wout", [256, 1024], F32, kind="ExternalInput").ap()
    o_p = nc.dram_tensor("o_part", [1024, 2, T], F32, kind="ExternalOutput").ap()

    import contextlib
    with contextlib.ExitStack() as es:
        E = es.enter_context
        d_sb = E(nc.sbuf_tensor("d_sb", [128, 2 * TC], F32))
        u_sb = E(nc.sbuf_tensor("u_sb", [128, 2 * TC], F32))
        z_sb = E(nc.sbuf_tensor("z_sb", [128, 2 * TC], F32))
        du_sb = E(nc.sbuf_tensor("du_sb", [128, 2 * TC], F32))
        b_sb = E(nc.sbuf_tensor("b_sb", [16, TC], F32))
        c_sb = E(nc.sbuf_tensor("c_sb", [16, TC], F32))
        bhat_sb = E(nc.sbuf_tensor("bhat_sb", [128, TC], F32))
        chat_sb = E(nc.sbuf_tensor("chat_sb", [128, TC], F32))
        dA_sb = E(nc.sbuf_tensor("dA_sb", [128, 2 * TC], F32))
        dBu_sb = E(nc.sbuf_tensor("dBu_sb", [128, 2 * TC], F32))
        h_sb = E(nc.sbuf_tensor("h_sb", [128, 2 * TC], F32))
        hc_sb = E(nc.sbuf_tensor("hc_sb", [128, 2 * TC], F32))
        y2_bf = E(nc.sbuf_tensor("y2_bf", [128, 2 * TC], BF16))
        szz_sb = E(nc.sbuf_tensor("szz_sb", [128, TC], F32))
        t1_sb = E(nc.sbuf_tensor("t1_sb", [128, TC], F32))
        stg = E(nc.sbuf_tensor("stg", [128, 2 * TC], F32))
        hcar_sb = E(nc.sbuf_tensor("hcar_sb", [128, 64], F32))
        e_sb = E(nc.sbuf_tensor("e_sb", [128, 16 * 128], F32))
        r_sb = E(nc.sbuf_tensor("r_sb", [128, 16 * 128], F32))
        rbc_sb = E(nc.sbuf_tensor("rbc_sb", [16, 128], F32))
        av_sb = E(nc.sbuf_tensor("av_sb", [128, 34], F32))
        wo_sb = E(nc.sbuf_tensor("wo_sb", [128, 2 * 1024], BF16))
        p0 = E(nc.psum_tensor("p0", [128, TC], F32))
        p1 = E(nc.psum_tensor("p1", [128, TC], F32))
        p3 = E(nc.psum_tensor("p3", [128, TC], F32))
        p4 = E(nc.psum_tensor("p4", [128, TC], F32))
        p2 = E(nc.psum_tensor("p2", [128, TC], F32))
        p5 = E(nc.psum_tensor("p5", [128, TC], F32))
        p6 = E(nc.psum_tensor("p6", [128, TC], F32))
        p7 = E(nc.psum_tensor("p7", [128, TC], F32))

        ds_sb = av_sb[:, 32:34]
        ld = DQ(ph, "sync", "ld")
        wld = DQ(ph, "gpsimd", "wld")
        st = DQ(ph, "gpsimd", "st")
        ch = Chain(ph, "ch")
        mm = Chain(ph, "mm")
        ev = Chain(ph, "ev")

        t, v, s, g, sy = (ph.eng["tensor"], ph.eng["vector"], ph.eng["scalar"],
                          ph.eng["gpsimd"], ph.eng["sync"])

        ld.start(e_sb[:, :], i_e[:, :], serialize=False)
        ld.start(r_sb[:, :], i_r[:, :], serialize=False)
        ld.start(rbc_sb[:, :], i_bc[:, :], serialize=False)
        ld.start(av_sb[:, 0:32], i_al.rearrange("(st p) -> p st", p=128),
                 serialize=False, allow_slow_non_contiguous=True)
        ld.start(ds_sb, i_ds[:, :], serialize=False)
        wld.start(wo_sb[:, :].rearrange("p (dt o) -> p dt o", dt=2),
                  i_wo.rearrange("(dt p) o -> p dt o", p=128), serialize=False)
        for e in ph.eng.values():
            ld.wait_all(e)
            wld.wait_all(e)
        ch.wait(s)
        ch.inc(s.activation(out=av_sb[:, 0:32], in_=av_sb[:, 0:32], func=AF.Exp))
        ch.wait(v)
        ch.inc(v.tensor_scalar(av_sb[:, 0:32], av_sb[:, 0:32], -1.0, None,
                               ALU.mult))
        ph.barrier()

        for b in range(2):
            ch.wait(v)
            ch.inc(v.memset(hcar_sb[:, 0:32], 0.0))
            ph.barrier()
            for c in range(NTC):
                t0 = c * TC
                sy.wait_ge(ch.sem, ch.n)
                sy.wait_ge(mm.sem, mm.n)
                for (sb, dr) in ((d_sb, i_d), (u_sb, i_u), (z_sb, i_z)):
                    ld.start(sb[:, :].rearrange("p (dt t) -> p dt t", dt=2),
                             dr.rearrange("(dt p) b t -> p dt b t", p=128)[
                                 :, :, b, t0:t0+TC])
                ld.start(b_sb[:, :], i_b[b, :, t0:t0+TC])
                ld.start(c_sb[:, :], i_c[b, :, t0:t0+TC])
                for e in (t, v, s):
                    ld.wait_all(e)
                    st.wait_all(e)
                for dt_i in range(2):
                    ch.wait(v)
                    ch.inc(v.tensor_tensor(du_sb[:, dt_i*TC:(dt_i+1)*TC],
                                           d_sb[:, dt_i*TC:(dt_i+1)*TC],
                                           u_sb[:, dt_i*TC:(dt_i+1)*TC],
                                           ALU.mult))
                ch.wait(t)
                mm.inc(t.matmul(p6[:, 0:TC], rbc_sb[:, :], b_sb[:, 0:TC],
                                start=True, stop=True))
                mm.inc(t.matmul(p7[:, 0:TC], rbc_sb[:, :], c_sb[:, 0:TC],
                                start=True, stop=True))
                s.wait_ge(mm.sem, mm.n)
                ch.wait(s)
                ch.inc(s.activation(out=bhat_sb[:, 0:TC], in_=p6[:, 0:TC],
                                    func=AF.Identity))
                ch.wait(s)
                ch.inc(s.activation(out=chat_sb[:, 0:TC], in_=p7[:, 0:TC],
                                    func=AF.Identity))
                bc_ready = ch.n

                exp_done = {}
                dbu_done = {}
                hc_done = {}
                scan_done = {}
                for dt_i in range(2):
                    for st16 in range(16):
                        sti = dt_i * 16 + st16
                        par = sti % 2
                        ps_d = p0 if par == 0 else p3
                        ps_du = p1 if par == 0 else p4
                        seg = lambda buf: buf[:, par*TC:(par+1)*TC]
                        if sti >= 2:
                            t.wait_ge(ch.sem, exp_done[sti-2])
                            t.wait_ge(ch.sem, dbu_done[sti-2])
                        i = t.matmul(ps_d[:, 0:TC], e_sb[:, st16*128:(st16+1)*128],
                                     d_sb[:, dt_i*TC:(dt_i+1)*TC],
                                     start=True, stop=True)
                        i = t.matmul(ps_du[:, 0:TC], e_sb[:, st16*128:(st16+1)*128],
                                     du_sb[:, dt_i*TC:(dt_i+1)*TC],
                                     start=True, stop=True)
                        mm.inc(i)
                        mm_here = mm.n
                        s.wait_ge(mm.sem, mm_here)
                        if sti >= 2:
                            s.wait_ge(ch.sem, scan_done[sti-2])
                        ch.wait(s)
                        ch.inc(s.activation(out=seg(dA_sb), in_=ps_d[:, 0:TC],
                                            func=AF.Exp,
                                            scale=av_sb[:, sti:sti+1]))
                        exp_done[sti] = ch.n
                        v.wait_ge(mm.sem, mm_here)
                        v.wait_ge(ch.sem, bc_ready)
                        if sti >= 2:
                            v.wait_ge(ch.sem, scan_done[sti-2])
                        ch.wait(v)
                        ch.inc(v.scalar_tensor_tensor(seg(dBu_sb), ps_du[:, 0:TC],
                                                      1.0, bhat_sb[:, 0:TC],
                                                      ALU.mult, ALU.mult))
                        dbu_done[sti] = ch.n
                        if sti >= 2:
                            v.wait_ge(ch.sem, hc_done[sti-2])
                        ch.wait(v)
                        ch.inc(v.tensor_tensor_scan(
                            seg(h_sb), seg(dA_sb), seg(dBu_sb),
                            hcar_sb[:, sti:sti+1], ALU.mult, ALU.add))
                        scan_done[sti] = ch.n
                        ch.wait(v)
                        ch.inc(v.tensor_copy(hcar_sb[:, sti:sti+1],
                                             seg(h_sb)[:, TC-1:TC]))
                        if sti >= 2:
                            v.wait_ge(mm.sem, mm.n)
                        ch.wait(v)
                        ch.inc(v.tensor_tensor(seg(hc_sb), seg(h_sb),
                                               chat_sb[:, 0:TC], ALU.mult))
                        hc_done[sti] = ch.n
                        t.wait_ge(ch.sem, hc_done[sti])
                        ps_y = p2 if dt_i == 0 else p5
                        i = t.matmul(ps_y[:, 0:TC], r_sb[:, st16*128:(st16+1)*128],
                                     hc_sb[:, par*TC:(par+1)*TC],
                                     start=(st16 == 0), stop=(st16 == 15))
                        mm.inc(i)
                for dt_i in range(2):
                    ps_y = p2 if dt_i == 0 else p5
                    ch.wait(s)
                    ch.inc(s.activation(out=szz_sb[:, 0:TC],
                                        in_=z_sb[:, dt_i*TC:(dt_i+1)*TC],
                                        func=AF.Silu))
                    v.wait_ge(mm.sem, mm.n)
                    ch.wait(v)
                    ch.inc(v.scalar_tensor_tensor(t1_sb[:, 0:TC],
                                                  u_sb[:, dt_i*TC:(dt_i+1)*TC],
                                                  ds_sb[:, dt_i:dt_i+1],
                                                  ps_y[:, 0:TC],
                                                  ALU.mult, ALU.add))
                    ch.wait(v)
                    ch.inc(v.tensor_tensor(y2_bf[:, dt_i*TC:(dt_i+1)*TC],
                                           t1_sb[:, 0:TC], szz_sb[:, 0:TC],
                                           ALU.mult))
                y2_ready = ch.n
                for ob in range(8):
                    par = ob % 2
                    ps_w = p6 if par == 0 else p7
                    t.wait_ge(ch.sem, y2_ready)
                    if ob >= 2:
                        t.wait_ge(ev.sem, ev.n - 1)
                    for dt_i in range(2):
                        i = t.matmul(ps_w[:, 0:TC],
                                     wo_sb[:, dt_i*1024 + ob*128:
                                           dt_i*1024 + (ob+1)*128],
                                     y2_bf[:, dt_i*TC:(dt_i+1)*TC],
                                     start=(dt_i == 0), stop=(dt_i == 1))
                    mm.inc(i)
                    s.wait_ge(mm.sem, mm.n)
                    s.wait_ge(st.sem, max(0, st.n - 16))
                    ev.inc(s.activation(out=stg[:, par*TC:(par+1)*TC],
                                        in_=ps_w[:, 0:TC], func=AF.Identity))
                    ch.wait(g)
                    g.wait_ge(ev.sem, ev.n)
                    st.start(o_p[ob*128:(ob+1)*128, b, t0:t0+TC],
                             stg[:, par*TC:(par+1)*TC])
                ph.barrier()

        for e in ph.eng.values():
            st.wait_all(e)
        ph.close()
    return nc



def emit_attn(nc, T=2048, first=False):
    TH = T + 513
    TQK = T + 512
    NC5 = TQK // 512
    NKT = 8
    ph = Phase(nc)
    if first:
        i_x = nc.dram_tensor("i_x", [1024, TH], F32, kind="ExternalInput").ap()
        i_p = nc.dram_tensor("i_parts", [8, 1024, TH], F32, kind="ExternalInput").ap()
        o_x1 = nc.dram_tensor("o_x1", [1024, T], F32, kind="ExternalOutput").ap()
        o_xpf = nc.dram_tensor("o_xpf", [1024, TH], F32, kind="ExternalOutput").ap()
    else:
        i_xp = nc.dram_tensor("i_xp", [1024, TH], F32, kind="ExternalInput").ap()
    wqkv = nc.dram_tensor("wqkv", [1024, 1536], F32, kind="ExternalInput").ap()
    wo = nc.dram_tensor("wo", [512, 1024], F32, kind="ExternalInput").ap()
    qsd = nc.dram_tensor("qsd", [128, 2], F32, kind="ExternalInput").ap()
    tabq = nc.dram_tensor("tabq", [128, 2 * T], BF16, kind="ExternalInput").ap()
    tabk = nc.dram_tensor("tabk", [128, 2 * TQK], BF16, kind="ExternalInput").ap()
    mgen = nc.dram_tensor("mgen", [128, 8 * 512], BF16, kind="ExternalInput").ap()
    m0 = nc.dram_tensor("m0", [128, 8 * 512], BF16, kind="ExternalInput").ap()
    identd = nc.dram_tensor("identd", [128, 128], BF16, kind="ExternalInput").ap()
    rotd = nc.dram_tensor("rotd", [128, 128], F32, kind="ExternalInput").ap()
    bo2d = nc.dram_tensor("bo2d", [128, 2], F32, kind="ExternalInput").ap()
    bc2d = nc.dram_tensor("bc2d", [2, 128], F32, kind="ExternalInput").ap()
    onesd = nc.dram_tensor("onesd", [128, 128], F32, kind="ExternalInput").ap()
    constsd = nc.dram_tensor("constsd", [1, 8], F32, kind="ExternalInput").ap()
    o_xa = nc.dram_tensor("o_xa", [1024, T], F32, kind="ExternalOutput").ap()

    import contextlib
    with contextlib.ExitStack() as es:
        E = es.enter_context
        xp_bf = E(nc.sbuf_tensor("xp_bf", [128, NKT * TH], BF16))
        qf_bf = E(nc.sbuf_tensor("qf_bf", [128, 4 * T], BF16))
        kf_bf = E(nc.sbuf_tensor("kf_bf", [128, 4 * TQK], BF16))
        vtok_bf = E(nc.sbuf_tensor("vtok_bf", [128, 4 * NC5 * 512], BF16))
        wq_buf = E(nc.sbuf_tensor("wq_buf", [128, 2 * NKT * 128], BF16))
        wo_bf = E(nc.sbuf_tensor("wo_bf", [128, 4 * 1024], BF16))
        tabq_sb = E(nc.sbuf_tensor("tabq_sb", [128, 2 * T], BF16))
        tabk_sb = E(nc.sbuf_tensor("tabk_sb", [128, 2 * TQK], BF16))
        mg_sb = E(nc.sbuf_tensor("mg_sb", [128, 8 * 512], BF16))
        m0_sb = E(nc.sbuf_tensor("m0_sb", [128, 8 * 512], BF16))
        id_sb = E(nc.sbuf_tensor("id_sb", [128, 128], BF16))
        rot_sb = E(nc.sbuf_tensor("rot_sb", [128, 128], F32))
        bo2_sb = E(nc.sbuf_tensor("bo2_sb", [128, 2], F32))
        bc2_sb = E(nc.sbuf_tensor("bc2_sb", [2, 128], F32))
        mslab = E(nc.sbuf_tensor("mslab", [128, NKT * 512], F32))
        xslab = E(nc.sbuf_tensor("xslab", [128, NKT * 512], F32))
        P_bf = E(nc.sbuf_tensor("P_bf", [128, 2 * 512], BF16))
        PT_bf = E(nc.sbuf_tensor("PT_bf", [128, 8 * 128], BF16))
        otok_bf = E(nc.sbuf_tensor("otok_bf", [128, 512], BF16))
        ofw_bf = E(nc.sbuf_tensor("ofw_bf", [128, 4 * 512], BF16))
        rinv_sb = E(nc.sbuf_tensor("rinv_sb", [128, 16], F32))
        rows2 = E(nc.sbuf_tensor("rows2", [2, 512], F32))
        scr1 = E(nc.sbuf_tensor("scr1", [128, 512], F32))
        scr2 = E(nc.sbuf_tensor("scr2", [128, 512], F32))
        scr3 = E(nc.sbuf_tensor("scr3", [128, 512], F32))
        par_sb = E(nc.sbuf_tensor("par_sb", [128, 8], F32))
        ones_row = E(nc.sbuf_tensor("ones_row", [1, 128], F32))
        rows = E(nc.sbuf_tensor("rows", [1, 4 * 512], F32))
        p0 = E(nc.psum_tensor("p0", [128, 512], F32))
        p1 = E(nc.psum_tensor("p1", [128, 512], F32))
        p2 = E(nc.psum_tensor("p2", [128, 512], F32))
        p3 = E(nc.psum_tensor("p3", [128, 512], F32))
        p4 = E(nc.psum_tensor("p4", [128, 512], F32))
        p5 = E(nc.psum_tensor("p5", [128, 512], F32))
        p6 = E(nc.psum_tensor("p6", [128, 512], F32))
        p7 = E(nc.psum_tensor("p7", [128, 512], F32))

        qs_col = par_sb[:, 0:1]
        ks_col = par_sb[:, 1:2]
        ones_col = par_sb[:, 2:3]
        eps_ap = par_sb[0:1, 3:4]
        mu1, rs1 = rows[0:1, 0:512], rows[0:1, 512:1024]
        scrow = rows[0:1, 1024:1536]

        ld = DQ(ph, "sync", "ld")
        ldg = DQ(ph, "gpsimd", "ldg")
        st = DQ(ph, "gpsimd", "st")
        st2 = DQ(ph, "sync", "st2")
        ch = Chain(ph, "ch")
        mm = Chain(ph, "mm")
        ev = Chain(ph, "ev")

        t, v, s, g, sy = (ph.eng["tensor"], ph.eng["vector"], ph.eng["scalar"],
                          ph.eng["gpsimd"], ph.eng["sync"])

        ldg.start(wo_bf[:, :].rearrange("p (kt o) -> p kt o", kt=4),
                  wo.rearrange("(kt p) o -> p kt o", p=128), serialize=False)
        ld.start(tabq_sb[:, :], tabq[:, :], serialize=False)
        ld.start(tabk_sb[:, :], tabk[:, :], serialize=False)
        ld.start(mg_sb[:, :], mgen[:, :], serialize=False)
        ld.start(m0_sb[:, :], m0[:, :], serialize=False)
        ld.start(id_sb[:, :], identd[:, :], serialize=False)
        ld.start(rot_sb[:, :], rotd[:, :], serialize=False)
        ld.start(bo2_sb[:, :], bo2d[:, :], serialize=False)
        ld.start(bc2_sb[:, :], bc2d[:, :], serialize=False)
        ld.start(par_sb[:, 0:2], qsd[:, :], serialize=False)
        ld.start(ones_row[0:1, :], onesd[0:1, :], serialize=False)
        ld.start(ones_col, onesd[:, 0:1], serialize=False,
                 allow_slow_non_contiguous=True)
        ld.start(eps_ap, constsd[0:1, 0:1], serialize=False,
                 allow_slow_non_contiguous=True)
        for e in ph.eng.values():
            ld.wait_all(e)
            ldg.wait_all(e)
        ph.barrier()

        pieces = [(0, 1)] + [(1 + 512*cc, 512) for cc in range(NC5)]
        for (c0, w0) in pieces:
            nck = {"allow_slow_non_contiguous": True} if w0 < 512 else {}
            if first:
                g.wait_ge(ch.sem, ch.n)
                ldg.start(mslab[:, :].rearrange("p (kt t) -> p kt t", kt=NKT)[:, :, 0:w0],
                          i_p.rearrange("j (kt p) t -> j p kt t", p=128)[
                              0, :, :, c0:c0+w0], **nck)
                for j in range(1, 8):
                    ldg.start(mslab[:, :].rearrange("p (kt t) -> p kt t", kt=NKT)[:, :, 0:w0],
                              i_p.rearrange("j (kt p) t -> j p kt t", p=128)[
                                  j, :, :, c0:c0+w0], accum_op=ALU.add, **nck)
                sy.wait_ge(ch.sem, ch.n)
                ld.start(xslab[:, :].rearrange("p (kt t) -> p kt t", kt=NKT)[:, :, 0:w0],
                         i_x.rearrange("(kt p) t -> p kt t", p=128)[:, :, c0:c0+w0], **nck)
                for e in (t, v, s):
                    ld.wait_all(e)
                    ldg.wait_all(e)
                    st.wait_all(e)
                    st2.wait_all(e)
                mkt = lambda kt: mslab[:, kt*512: kt*512 + w0]
                xkt = lambda kt: xslab[:, kt*512: kt*512 + w0]
                ch.wait(t); ch.wait(s)
                ln_stats(ph, ch, [mkt(kt) for kt in range(NKT)], w0, p6, p7,
                         scr1, ones_col, mu1, rs1, scrow, eps_ap=eps_ap)
                bcast_rows(ph, ch, [(mu1, p6), (rs1, p7)], w0, ones_row)
                ch.wait(v)
                for kt in range(NKT):
                    ch.wait(v)
                    ch.inc(v.tensor_tensor(scr1[:, 0:w0], mkt(kt), p6[:, 0:w0],
                                           ALU.subtract))
                    ch.wait(v)
                    ch.inc(v.tensor_tensor(scr2[:, 0:w0], scr1[:, 0:w0],
                                           p7[:, 0:w0], ALU.mult))
                    ch.wait(v)
                    ch.inc(v.tensor_tensor(mkt(kt), scr2[:, 0:w0], xkt(kt),
                                           ALU.add))
                if c0 >= 513:
                    ch.wait(sy)
                    st2.start(o_x1.rearrange("(kt p) t -> p kt t", p=128)[
                                  :, :, c0-513: c0-513+w0],
                              mslab[:, :].rearrange("p (kt t) -> p kt t",
                                                    kt=NKT)[:, :, 0:w0])
                ch.wait(t); ch.wait(s)
                ln_stats(ph, ch, [mkt(kt) for kt in range(NKT)], w0, p6, p7,
                         scr1, ones_col, mu1, rs1, scrow, eps_ap=eps_ap)
                bcast_rows(ph, ch, [(mu1, p6), (rs1, p7)], w0, ones_row)
                ch.wait(v)
                for kt in range(NKT):
                    ch.wait(v)
                    ch.inc(v.tensor_tensor(scr1[:, 0:w0], mkt(kt), p6[:, 0:w0],
                                           ALU.subtract))
                    ch.wait(v)
                    ch.inc(v.tensor_tensor(scr2[:, 0:w0], scr1[:, 0:w0],
                                           p7[:, 0:w0], ALU.mult))
                    ch.wait(v)
                    ch.inc(v.tensor_copy(xp_bf[:, kt*TH + c0: kt*TH + c0 + w0],
                                         scr2[:, 0:w0]))
                    ch.wait(v)
                    ch.inc(v.tensor_copy(xslab[:, kt*512: kt*512 + w0],
                                         scr2[:, 0:w0]))
                ch.wait(sy)
                st2.start(o_xpf.rearrange("(kt p) t -> p kt t", p=128)[
                              :, :, c0:c0+w0],
                          xslab[:, :].rearrange("p (kt t) -> p kt t",
                                                kt=NKT)[:, :, 0:w0], **nck)
            else:
                sy.wait_ge(ch.sem, ch.n)
                ld.start(xslab[:, :].rearrange("p (kt t) -> p kt t", kt=NKT)[:, :, 0:w0],
                         i_xp.rearrange("(kt p) t -> p kt t", p=128)[:, :, c0:c0+w0], **nck)
                for e in (v,):
                    ld.wait_all(e)
                for kt in range(NKT):
                    ch.wait(v)
                    ch.inc(v.tensor_copy(xp_bf[:, kt*TH + c0: kt*TH + c0 + w0],
                                         xslab[:, kt*512: kt*512 + w0]))
            ph.barrier()

        def qknorm(ob, c5, is_q, psrc):
            s.wait_ge(mm.sem, mm.n)
            ch.wait(s)
            ch.inc(s.activation(out=scr1[:, 0:512], in_=psrc[:, 0:512],
                                func=AF.Identity))
            ch.wait(s)
            ch.inc(s.activation(out=scr2[:, 0:512], in_=scr1[:, 0:512],
                                func=AF.Square))
            ch.wait(t)
            mm.inc(t.matmul(p6[0:2, 0:512], bo2_sb[:, 0:2], scr2[:, 0:512],
                            start=True, stop=True))
            s.wait_ge(mm.sem, mm.n)
            ch.wait(s)
            ch.inc(s.activation(out=rows2[0:2, 0:512],
                                in_=p6[0:2, 0:512], func=AF.Sqrt))
            ch.wait(v)
            ch.inc(v.tensor_scalar(rows2[0:2, 0:512], rows2[0:2, 0:512],
                                   1e-12, None, ALU.max))
            ch.wait(v)
            ch.inc(v.reciprocal(rows2[0:2, 0:512], rows2[0:2, 0:512]))
            ch.wait(t)
            mm.inc(t.matmul(p7[:, 0:512], bc2_sb[:, :],
                            rows2[0:2, 0:512], start=True, stop=True))
            v.wait_ge(mm.sem, mm.n)
            ch.wait(v)
            ch.inc(v.scalar_tensor_tensor(scr2[:, 0:512], scr1[:, 0:512],
                                          qs_col if is_q else ks_col,
                                          p7[:, 0:512], ALU.mult, ALU.mult))
            ch.wait(t)
            mm.inc(t.matmul(p6[:, 0:512], rot_sb[:, :], scr2[:, 0:512],
                            start=True, stop=True))
            if is_q:
                tcos = tabq_sb[:, (c5-1)*512: c5*512]
                tsin = tabq_sb[:, T + (c5-1)*512: T + c5*512]
                dst = qf_bf[:, ob*T + (c5-1)*512: ob*T + c5*512]
            else:
                tcos = tabk_sb[:, c5*512: (c5+1)*512]
                tsin = tabk_sb[:, TQK + c5*512: TQK + (c5+1)*512]
                dst = kf_bf[:, ob*TQK + c5*512: ob*TQK + (c5+1)*512]
            ch.wait(v)
            ch.inc(v.tensor_tensor(scr3[:, 0:512], scr2[:, 0:512], tcos,
                                   ALU.mult))
            v.wait_ge(mm.sem, mm.n)
            ch.wait(v)
            ch.inc(v.scalar_tensor_tensor(scr2[:, 0:512], p6[:, 0:512], 1.0,
                                          tsin, ALU.mult, ALU.mult))
            ch.wait(v)
            ch.inc(v.tensor_tensor(dst, scr3[:, 0:512], scr2[:, 0:512],
                                   ALU.add))

        for c5 in range(NC5):
            xc0 = 1 + c5 * 512
            for ob in range(12):
                if ob < 4 and c5 == 0:
                    continue
                slot = ob % 2
                g.wait_ge(mm.sem, mm.n)
                ldg.start(wq_buf[:, slot*NKT*128:(slot+1)*NKT*128].rearrange(
                              "p (kt o) -> p kt o", kt=NKT),
                          wqkv.rearrange("(kt p) o -> p kt o", p=128)[
                              :, :, ob*128:(ob+1)*128])
                t.wait_ge(ldg.sem, ldg.n)
                t.wait_ge(ev.sem, ev.n)
                for kt in range(NKT):
                    off = xc0 if kt < 4 else xc0 - 1
                    i = t.matmul(p0[:, 0:512],
                                 wq_buf[:, slot*NKT*128 + kt*128:
                                        slot*NKT*128 + (kt+1)*128],
                                 xp_bf[:, kt*TH + off: kt*TH + off + 512],
                                 start=(kt == 0), stop=(kt == NKT-1))
                mm.inc(i)
                if ob < 4:
                    qknorm(ob, c5, True, p0)
                    ev.inc(v.nop())
                elif ob < 8:
                    qknorm(ob - 4, c5, False, p0)
                    ev.inc(v.nop())
                else:
                    s.wait_ge(mm.sem, mm.n)
                    ch.wait(s)
                    ch.inc(s.activation(out=P_bf[:, 0:512], in_=p0[:, 0:512],
                                        func=AF.Identity))
                    ch.wait(t)
                    for js in range(4):
                        t.wait_ge(ch.sem, ch.n)
                        i = t.transpose(p1.bitcast(BF16)[:, 0:128],
                                        P_bf[:, js*128:(js+1)*128], id_sb[:, :])
                        mm.inc(i)
                        s.wait_ge(mm.sem, mm.n)
                        ch.wait(s)
                        ch.inc(s.activation(
                            out=vtok_bf[:, (c5*4+js)*512 + (ob-8)*128:
                                        (c5*4+js)*512 + (ob-7)*128],
                            in_=p1.bitcast(BF16)[:, 0:128], func=AF.Identity))
                    ev.inc(v.nop())
            ph.barrier()

        NW = T // 512
        for wq in range(NW):
            mask = m0_sb if wq == 0 else mg_sb
            for bi in range(4):
                ops = p4
                for h in range(8):
                    hb = 64 * (h % 2)
                    ht = h // 2
                    t.wait_ge(ch.sem, ch.n)
                    t.wait_ge(ev.sem, ev.n)
                    lq = qf_bf[hb:hb+64,
                               ht*T + wq*512 + bi*128: ht*T + wq*512 + (bi+1)*128]
                    mm.inc(t.matmul(p0[:, 0:512], lq,
                                    kf_bf[hb:hb+64,
                                          ht*TQK + wq*512: ht*TQK + (wq+1)*512],
                                    start=True, stop=True))
                    mm.inc(t.matmul(p1[:, 0:512], lq,
                                    kf_bf[hb:hb+64,
                                          ht*TQK + (wq+1)*512: ht*TQK + (wq+2)*512],
                                    start=True, stop=True))
                    v.wait_ge(mm.sem, mm.n)
                    ch.wait(v)
                    ch.inc(v.tensor_tensor(p0[:, 0:512], p0[:, 0:512],
                                           mask[:, (bi*2+0)*512:(bi*2+1)*512],
                                           ALU.add))
                    ch.wait(v)
                    ch.inc(v.tensor_tensor(p1[:, 0:512], p1[:, 0:512],
                                           mask[:, (bi*2+1)*512:(bi*2+2)*512],
                                           ALU.add))
                    ch.wait(v)
                    ch.inc(v.reduce_max(scr1[:, 0:1], p0[:, 0:512],
                                        axis=mybir.AxisListType.X))
                    ch.wait(v)
                    ch.inc(v.reduce_max(scr1[:, 1:2], p1[:, 0:512],
                                        axis=mybir.AxisListType.X))
                    ch.wait(v)
                    ch.inc(v.tensor_tensor(scr1[:, 2:3], scr1[:, 0:1],
                                           scr1[:, 1:2], ALU.max))
                    ch.wait(v)
                    ch.inc(v.tensor_scalar(scr1[:, 3:4], scr1[:, 2:3], -1.0,
                                           None, ALU.mult))
                    s.wait_ge(mm.sem, mm.n)
                    ch.wait(s)
                    ch.inc(s.activation(out=P_bf[:, 0:512], in_=p0[:, 0:512],
                                        func=AF.Exp, bias=scr1[:, 3:4],
                                        accum_out=scr1[:, 4:5]))
                    ch.wait(s)
                    ch.inc(s.activation(out=P_bf[:, 512:1024], in_=p1[:, 0:512],
                                        func=AF.Exp, bias=scr1[:, 3:4],
                                        accum_out=scr1[:, 5:6]))
                    ch.wait(v)
                    ch.inc(v.tensor_tensor(scr1[:, 6:7], scr1[:, 4:5],
                                           scr1[:, 5:6], ALU.add))
                    ch.wait(v)
                    ch.inc(v.reciprocal(rinv_sb[:, h:h+1], scr1[:, 6:7]))
                    ch.wait(t)
                    for js in range(8):
                        t.wait_ge(ch.sem, ch.n)
                        i = t.transpose(p2.bitcast(BF16)[:, 0:128],
                                        P_bf[:, js*128:(js+1)*128], id_sb[:, :])
                        mm.inc(i)
                        s.wait_ge(mm.sem, mm.n)
                        ch.wait(s)
                        ch.inc(s.activation(out=PT_bf[:, js*128:(js+1)*128],
                                            in_=p2.bitcast(BF16)[:, 0:128],
                                            func=AF.Identity))
                    t.wait_ge(ch.sem, ch.n)
                    for js in range(8):
                        jtg = wq*4 + js
                        i = t.matmul(ops[:, 64*h:64*h+64],
                                     PT_bf[:, js*128:(js+1)*128],
                                     vtok_bf[:, jtg*512 + 64*h: jtg*512 + 64*h + 64],
                                     start=(js == 0), stop=(js == 7))
                    mm.inc(i)
                v.wait_ge(mm.sem, mm.n)
                for h in range(8):
                    ch.wait(v)
                    ch.inc(v.tensor_scalar(otok_bf[:, 64*h:64*h+64],
                                           ops[:, 64*h:64*h+64],
                                           rinv_sb[:, h:h+1], None, ALU.mult))
                ch.wait(t)
                for et in range(4):
                    t.wait_ge(ch.sem, ch.n)
                    i = t.transpose(p3.bitcast(BF16)[:, 0:128],
                                    otok_bf[:, et*128:(et+1)*128], id_sb[:, :])
                    mm.inc(i)
                    s.wait_ge(mm.sem, mm.n)
                    ch.wait(s)
                    ch.inc(s.activation(out=ofw_bf[:, et*512 + bi*128:
                                                   et*512 + (bi+1)*128],
                                        in_=p3.bitcast(BF16)[:, 0:128],
                                        func=AF.Identity))
                ev.inc(s.nop())
            src = o_xpf if first else i_xp
            sy.wait_ge(ch.sem, ch.n)
            st.wait_all(sy)
            ld.start(xslab[:, :].rearrange("p (kt t) -> p kt t", kt=NKT)[:, :, 0:512],
                     src.rearrange("(kt p) t -> p kt t", p=128)[
                         :, :, 513 + wq*512: 513 + (wq+1)*512])
            for e in (t, v, s):
                ld.wait_all(e)
                st.wait_all(e)
            t.wait_ge(ch.sem, ch.n)
            add_done = {}
            for ob in range(NKT):
                par = ob % 2
                psw = p5 if par == 0 else p6
                if ob >= 2:
                    t.wait_ge(ch.sem, add_done[ob - 2])
                for et in range(4):
                    i = t.matmul(psw[:, 0:512],
                                 wo_bf[:, et*1024 + ob*128: et*1024 + (ob+1)*128],
                                 ofw_bf[:, et*512:(et+1)*512],
                                 start=(et == 0), stop=(et == 3))
                mm.inc(i)
                v.wait_ge(mm.sem, mm.n)
                v.wait_ge(st.sem, max(0, st.n - 16))
                ch.wait(v)
                ch.inc(v.tensor_tensor(xslab[:, ob*512:(ob+1)*512],
                                       psw[:, 0:512],
                                       xslab[:, ob*512:(ob+1)*512], ALU.add))
                add_done[ob] = ch.n
                ch.wait(g)
                g.wait_ge(ch.sem, ch.n)
                st.start(o_xa.rearrange("(kt p) t -> p kt t", p=128)[
                             :, ob, wq*512:(wq+1)*512],
                         xslab[:, ob*512:(ob+1)*512])
            ph.barrier()

        for e in ph.eng.values():
            st.wait_all(e)
            st2.wait_all(e)
        ph.close()
    return nc



def emit_geglu(nc, T=2048, last=False):
    NKT = 8
    HB = 22
    HBW = [128] * 21 + [42]
    NTC = T // 512
    TC = 512
    ph = Phase(nc)
    i_xa = nc.dram_tensor("i_xa", [1024, T + 1], F32, kind="ExternalInput").ap()
    w1 = nc.dram_tensor("w1", [1024, 5460], F32, kind="ExternalInput").ap()
    w2 = nc.dram_tensor("w2", [2730, 1024], F32, kind="ExternalInput").ap()
    b1ad = nc.dram_tensor("b1ad", [128, HB], F32, kind="ExternalInput").ap()
    b1gd = nc.dram_tensor("b1gd", [128, HB], F32, kind="ExternalInput").ap()
    lgd = nc.dram_tensor("lgd", [128, 8], F32, kind="ExternalInput").ap()
    lbd = nc.dram_tensor("lbd", [128, 8], F32, kind="ExternalInput").ap()
    b2d = nc.dram_tensor("b2d", [128, 8], F32, kind="ExternalInput").ap()
    onesd = nc.dram_tensor("onesd", [128, 128], F32, kind="ExternalInput").ap()
    constsd = nc.dram_tensor("constsd", [1, 8], F32, kind="ExternalInput").ap()
    if last:
        i_x1 = nc.dram_tensor("i_x1", [1024, T], F32, kind="ExternalInput").ap()
    o_out = nc.dram_tensor("o_out", [1024, T], F32, kind="ExternalOutput").ap()

    import contextlib
    with contextlib.ExitStack() as es:
        E = es.enter_context
        xa_sb = E(nc.sbuf_tensor("xa_sb", [128, NKT * (TC + 1)], F32))
        x1_sb = E(nc.sbuf_tensor("x1_sb", [128, NKT * TC], F32))
        tgt_bf = E(nc.sbuf_tensor("tgt_bf", [128, NKT * TC], BF16))
        h_bf = E(nc.sbuf_tensor("h_bf", [128, HB * TC], BF16))
        out_sb = E(nc.sbuf_tensor("out_sb", [128, NKT * TC], F32))
        w1buf = E(nc.sbuf_tensor("w1buf", [128, 2 * 2 * NKT * 128], BF16))
        w2buf = E(nc.sbuf_tensor("w2buf", [128, 2 * 128], BF16))
        scr1 = E(nc.sbuf_tensor("scr1", [128, TC], F32))
        scr2 = E(nc.sbuf_tensor("scr2", [128, TC], F32))
        scr3 = E(nc.sbuf_tensor("scr3", [128, TC], F32))
        par_sb = E(nc.sbuf_tensor("par_sb", [128, 2 * HB + 26], F32))
        ones_row = E(nc.sbuf_tensor("ones_row", [1, 128], F32))
        rows = E(nc.sbuf_tensor("rows", [1, 4 * TC], F32))
        p0 = E(nc.psum_tensor("p0", [128, TC], F32))
        p1 = E(nc.psum_tensor("p1", [128, TC], F32))
        p2 = E(nc.psum_tensor("p2", [128, TC], F32))
        p3 = E(nc.psum_tensor("p3", [128, TC], F32))
        p6 = E(nc.psum_tensor("p6", [128, TC], F32))
        p7 = E(nc.psum_tensor("p7", [128, TC], F32))

        b1a_sb = par_sb[:, 0:HB]
        b1g_sb = par_sb[:, HB:2*HB]
        lg_sb = par_sb[:, 2*HB:2*HB+8]
        lb_sb = par_sb[:, 2*HB+8:2*HB+16]
        b2_sb = par_sb[:, 2*HB+16:2*HB+24]
        ones_col = par_sb[:, 2*HB+24:2*HB+25]
        eps_ap = par_sb[0:1, 2*HB+25:2*HB+26]
        mu1, rs1 = rows[0:1, 0:TC], rows[0:1, TC:2*TC]
        scrow = rows[0:1, 2*TC:3*TC]

        ld = DQ(ph, "sync", "ld")
        wld = [DQ(ph, "gpsimd", "wld0"), DQ(ph, "gpsimd", "wld1")]
        st = DQ(ph, "gpsimd", "st")
        ch = Chain(ph, "ch")
        mm = Chain(ph, "mm")

        t, v, s, g, sy = (ph.eng["tensor"], ph.eng["vector"], ph.eng["scalar"],
                          ph.eng["gpsimd"], ph.eng["sync"])

        ld.start(b1a_sb, b1ad[:, :], serialize=False)
        ld.start(b1g_sb, b1gd[:, :], serialize=False)
        ld.start(lg_sb, lgd[:, :], serialize=False)
        ld.start(lb_sb, lbd[:, :], serialize=False)
        ld.start(b2_sb, b2d[:, :], serialize=False)
        ld.start(ones_row[0:1, :], onesd[0:1, :], serialize=False)
        ld.start(ones_col, onesd[:, 0:1], serialize=False,
                 allow_slow_non_contiguous=True)
        ld.start(eps_ap, constsd[0:1, 0:1], serialize=False,
                 allow_slow_non_contiguous=True)
        for e in ph.eng.values():
            ld.wait_all(e)

        for c in range(NTC):
            t0 = c * TC
            sy.wait_ge(ch.sem, ch.n)
            sy.wait_ge(mm.sem, mm.n)
            st.wait_all(sy)
            ld.start(xa_sb[:, :].rearrange("p (kt t) -> p kt t", kt=NKT),
                     i_xa.rearrange("(kt p) t -> p kt t", p=128)[
                         :, :, t0: t0 + TC + 1])
            if last:
                ld.start(x1_sb[:, :].rearrange("p (kt t) -> p kt t", kt=NKT),
                         i_x1.rearrange("(kt p) t -> p kt t", p=128)[
                             :, :, t0: t0 + TC])
            for e in (t, v, s):
                ld.wait_all(e)
                st.wait_all(e)
            def xsh(kt):
                off = 1 if kt < 4 else 0
                return xa_sb[:, kt*(TC+1)+off: kt*(TC+1)+off+TC]
            def xcur(kt):
                return xa_sb[:, kt*(TC+1)+1: kt*(TC+1)+1+TC]
            ch.wait(t); ch.wait(s)
            ln_stats(ph, ch, [xsh(kt) for kt in range(NKT)], TC, p6, p7,
                     scr1, ones_col, mu1, rs1, scrow, eps_ap=eps_ap)
            bcast_rows(ph, ch, [(mu1, p6), (rs1, p7)], TC, ones_row)
            ch.wait(v)
            act_done = {}
            for kt in range(NKT):
                scr_o = scr2 if kt % 2 == 0 else scr3
                ch.wait(v)
                if kt >= 2:
                    v.wait_ge(ch.sem, act_done[kt-2])
                ch.inc(v.tensor_tensor(scr1[:, 0:TC], xsh(kt), p6[:, 0:TC],
                                       ALU.subtract))
                ch.wait(v)
                ch.inc(v.tensor_tensor(scr_o[:, 0:TC], scr1[:, 0:TC],
                                       p7[:, 0:TC], ALU.mult))
                s.wait_ge(ch.sem, ch.n)
                ch.inc(s.activation(out=tgt_bf[:, kt*TC:(kt+1)*TC],
                                    in_=scr_o[:, 0:TC], func=AF.Identity,
                                    scale=lg_sb[:, kt:kt+1],
                                    bias=lb_sb[:, kt:kt+1]))
                act_done[kt] = ch.n
            tgt_ready = ch.n

            h_done = {}
            wq_base = [wld[0].n, wld[1].n]
            mm_base_h = mm.n
            for hb in range(HB):
                slot = hb % 2
                wslab = w1buf[:, slot*2*NKT*128:(slot+1)*2*NKT*128]
                if hb >= 2:
                    g.wait_ge(mm.sem, mm_base_h + hb - 1)
                wld[slot].start(
                    wslab[:, 0:NKT*128].rearrange("p (kt o) -> p kt o",
                                                  kt=NKT)[:, :, 0:HBW[hb]],
                    w1.rearrange("(kt p) o -> p kt o", p=128)[
                        :, :, hb*128: hb*128 + HBW[hb]])
                wld[slot].start(
                    wslab[:, NKT*128:2*NKT*128].rearrange("p (kt o) -> p kt o",
                                                          kt=NKT)[:, :, 0:HBW[hb]],
                    w1.rearrange("(kt p) o -> p kt o", p=128)[
                        :, :, 2730 + hb*128: 2730 + hb*128 + HBW[hb]])
            for hb in range(HB):
                slot = hb % 2
                bw = HBW[hb]
                ps_a = p0 if slot == 0 else p2
                ps_g = p1 if slot == 0 else p3
                wslab = w1buf[:, slot*2*NKT*128:(slot+1)*2*NKT*128]
                t.wait_ge(ch.sem, tgt_ready)
                t.wait_ge(wld[slot].sem, wq_base[slot] + 32*(hb//2 + 1))
                if hb >= 2:
                    t.wait_ge(ch.sem, h_done[hb-2])
                for kt in range(NKT):
                    t.matmul(ps_a[0:bw, 0:TC],
                             wslab[:, kt*128: kt*128 + bw],
                             tgt_bf[:, kt*TC:(kt+1)*TC],
                             start=(kt == 0), stop=(kt == NKT-1))
                for kt in range(NKT):
                    i = t.matmul(ps_g[0:bw, 0:TC],
                                 wslab[:, NKT*128 + kt*128: NKT*128 + kt*128 + bw],
                                 tgt_bf[:, kt*TC:(kt+1)*TC],
                                 start=(kt == 0), stop=(kt == NKT-1))
                mm.inc(i)
                s.wait_ge(mm.sem, mm.n)
                ch.wait(s)
                ch.inc(s.activation(out=scr1[0:bw, 0:TC], in_=ps_a[0:bw, 0:TC],
                                    func=AF.Identity, bias=b1a_sb[0:bw, hb:hb+1]))
                ch.wait(s)
                ch.inc(s.activation(out=scr2[0:bw, 0:TC], in_=ps_g[0:bw, 0:TC],
                                    func=AF.Gelu, bias=b1g_sb[0:bw, hb:hb+1]))
                ch.wait(v)
                ch.inc(v.tensor_tensor(h_bf[0:bw, hb*TC:(hb+1)*TC],
                                       scr1[0:bw, 0:TC], scr2[0:bw, 0:TC],
                                       ALU.mult))
                h_done[hb] = ch.n
            h_ready = ch.n

            add_done = {}
            w2_base = [wld[0].n, wld[1].n]
            mm_base_w2 = mm.n
            ldi = 0
            for ob in range(NKT):
                for hb in range(HB):
                    slot = ldi % 2
                    if ldi >= 2:
                        g.wait_ge(mm.sem, mm_base_w2 + ldi - 1)
                    wld[slot].start(w2buf[:, slot*128:(slot+1)*128][0:HBW[hb], :],
                                    w2[hb*128: hb*128 + HBW[hb],
                                       ob*128:(ob+1)*128])
                    ldi += 1
            ldi = 0
            for ob in range(NKT):
                par = ob % 2
                psw = p2 if par == 0 else p3
                t.wait_ge(ch.sem, h_ready)
                if ob >= 2:
                    t.wait_ge(ch.sem, add_done[ob-2])
                for hb in range(HB):
                    slot = ldi % 2
                    bw = HBW[hb]
                    t.wait_ge(wld[slot].sem, w2_base[slot] + 16*(ldi//2 + 1))
                    mm.inc(t.matmul(psw[:, 0:TC],
                                    w2buf[:, slot*128:(slot+1)*128][0:bw, :],
                                    h_bf[0:bw, hb*TC:(hb+1)*TC],
                                    start=(hb == 0), stop=(hb == HB-1)))
                    ldi += 1
                s.wait_ge(mm.sem, mm.n)
                ch.wait(s)
                ch.inc(s.activation(out=scr1[:, 0:TC], in_=psw[:, 0:TC],
                                    func=AF.Identity, bias=b2_sb[:, ob:ob+1]))
                ch.wait(v)
                ch.inc(v.tensor_tensor(out_sb[:, ob*TC:(ob+1)*TC],
                                       scr1[:, 0:TC], xcur(ob), ALU.add))
                add_done[ob] = ch.n
            if not last:
                ch.wait(g)
                st.start(o_out.rearrange("(kt p) t -> p kt t", p=128)[
                             :, :, t0:t0+TC],
                         out_sb[:, :].rearrange("p (kt t) -> p kt t", kt=NKT))
            else:
                ch.wait(t); ch.wait(s)
                ln_stats(ph, ch, [out_sb[:, kt*TC:(kt+1)*TC] for kt in range(NKT)],
                         TC, p6, p7, scr1, ones_col, mu1, rs1, scrow,
                         eps_ap=eps_ap)
                bcast_rows(ph, ch, [(mu1, p6), (rs1, p7)], TC, ones_row)
                ch.wait(v)
                for kt in range(NKT):
                    ch.wait(v)
                    ch.inc(v.tensor_tensor(scr1[:, 0:TC],
                                           out_sb[:, kt*TC:(kt+1)*TC],
                                           p6[:, 0:TC], ALU.subtract))
                    ch.wait(v)
                    ch.inc(v.tensor_tensor(scr2[:, 0:TC], scr1[:, 0:TC],
                                           p7[:, 0:TC], ALU.mult))
                    ch.wait(v)
                    ch.inc(v.tensor_tensor(out_sb[:, kt*TC:(kt+1)*TC],
                                           scr2[:, 0:TC],
                                           x1_sb[:, kt*TC:(kt+1)*TC], ALU.add))
                ch.wait(g)
                st.start(o_out.rearrange("(kt p) t -> p kt t", p=128)[
                             :, :, t0:t0+TC],
                         out_sb[:, :].rearrange("p (kt t) -> p kt t", kt=NKT))
            ph.barrier()

        for e in ph.eng.values():
            st.wait_all(e)
        ph.close()
    return nc



import ml_dtypes
from concourse.bass_utils import run_bass_kernel_spmd

_T = 2048
_NG = 8192
_CORES = list(range(8))


def _ones128():
    return np.ones((128, 128), np.float32)


def _consts():
    return np.array([[1e-5, 0, 0, 0, 0, 0, 0, 0]], np.float32)


def _sliceT(full_b, start, width):
    out = np.zeros((1024, width), np.float32)
    s0 = max(0, start)
    pad = s0 - start
    out[:, pad:] = full_b[s0:start + width].T
    return np.ascontiguousarray(out)


def _mk_rot_tables():
    half = np.arange(0, 64, 2, dtype=np.float32)
    inv_freq = 1.0 / 10000.0 ** (half / 64)
    base = (half + 0.4 * 64) / (1.4 * 64)

    def tables(pos):
        fr = pos[:, None] * inv_freq[None, :]
        fr = np.concatenate([fr, fr], -1)
        power = (pos - _NG // 2) / 256.0
        sc = base[None, :] ** power[:, None]
        sc = np.concatenate([sc, sc], -1)
        return np.cos(fr), np.sin(fr), sc
    return tables


def _mk_masks(first_window_special):
    m = np.zeros((128, 8 * 512), np.float32)
    j = np.arange(512)
    for bi in range(4):
        for chunk in range(2):
            blk = np.zeros((128, 512), np.float32)
            for p in range(128):
                i = 128 * bi + p
                j1024 = chunk * 512 + j
                ok = (j1024 >= i) & (j1024 <= i + 512)
                if first_window_special and chunk == 0:
                    ok = np.zeros(512, bool)
                blk[p] = np.where(ok, 0.0, -1e30)
            m[:, (bi * 2 + chunk) * 512:(bi * 2 + chunk + 1) * 512] = blk
    return m.astype(ml_dtypes.bfloat16)


def _packr(vec, widths):
    out = np.zeros((128, len(widths)), np.float32)
    off = 0
    for i, w in enumerate(widths):
        out[:w, i] = vec[off:off + w]
        off += w
    return out


def kernel(**ins):
    x0 = np.asarray(ins["L0_attn_x"], np.float32)
    ones = _ones128()
    consts = _consts()

    nc = bass.Bass("TRN2", target_bir_lowering=False)
    emit_mamba_fe(nc, _T)
    cw = np.zeros((128, 64), np.float32)
    conv_w = ins["L0_attn_conv_w"][:, 0, :]
    for ob in range(16):
        for k in range(4):
            cw[:, 4 * ob + k] = conv_w[ob * 128:(ob + 1) * 128, k]
    maps = []
    for c in _CORES:
        b, q = c // 4, c % 4
        maps.append({
            "xT": _sliceT(x0[b], q * _T - 3, _T + 3),
            "w_in": ins["L0_attn_W_in"], "w_xp": ins["L0_attn_W_xproj"],
            "w_dt": ins["L0_attn_W_dt"], "cwd": cw,
            "cbd": pack_vec(ins["L0_attn_conv_b"], 16),
            "bdtd": pack_vec(ins["L0_attn_b_dt"], 16),
            "onesd": ones, "constsd": consts,
        })
    r1 = run_bass_kernel_spmd(nc, maps, core_ids=_CORES)

    def assem_ch(key, nch):
        out = np.zeros((nch, 2, _NG), np.float32)
        for c in _CORES:
            b, q = c // 4, c % 4
            out[:, b, q * _T:(q + 1) * _T] = r1.results[c][key]
        return out
    dT = assem_ch("o_dT", 2048)
    uT = assem_ch("o_uT", 2048)
    zT = assem_ch("o_zT", 2048)
    BT = np.zeros((2, 16, _NG), np.float32)
    CT = np.zeros((2, 16, _NG), np.float32)
    for c in _CORES:
        b, q = c // 4, c % 4
        BT[b, :, q * _T:(q + 1) * _T] = r1.results[c]["o_BT"]
        CT[b, :, q * _T:(q + 1) * _T] = r1.results[c]["o_CT"]

    nc = bass.Bass("TRN2", target_bir_lowering=False)
    emit_mamba_scan(nc, _NG)
    E = np.zeros((128, 16 * 128), np.float32)
    R = np.zeros((128, 16 * 128), np.float32)
    for st16 in range(16):
        for p in range(128):
            j = p // 16
            E[8 * st16 + j, st16 * 128 + p] = 1.0
            R[p, st16 * 128 + 8 * st16 + j] = 1.0
    RepBC = np.zeros((16, 128), np.float32)
    for p in range(128):
        RepBC[p % 16, p] = 1.0
    maps = []
    for c in _CORES:
        ds = ins["L0_attn_D_skip"][c * 256:(c + 1) * 256]
        maps.append({
            "i_dT": np.ascontiguousarray(dT[c * 256:(c + 1) * 256]),
            "i_uT": np.ascontiguousarray(uT[c * 256:(c + 1) * 256]),
            "i_zT": np.ascontiguousarray(zT[c * 256:(c + 1) * 256]),
            "i_BT": BT, "i_CT": CT,
            "i_Alog": np.ascontiguousarray(
                ins["L0_attn_A_log"][c * 256:(c + 1) * 256].reshape(-1)),
            "i_E": E, "i_R": R, "i_RepBC": RepBC,
            "i_Dskip": np.ascontiguousarray(np.stack([ds[:128], ds[128:]], 1)),
            "i_wout": np.ascontiguousarray(
                ins["L0_attn_W_out"][c * 256:(c + 1) * 256]),
        })
    r2 = run_bass_kernel_spmd(nc, maps, core_ids=_CORES)
    parts = np.stack([r2.results[c]["o_part"] for c in _CORES])

    tables = _mk_rot_tables()
    ident = np.eye(128).astype(ml_dtypes.bfloat16)
    rot = np.zeros((128, 128), np.float32)
    for blk in range(2):
        for m_ in range(64):
            if m_ < 32:
                rot[blk * 64 + m_ + 32, blk * 64 + m_] = -1.0
            else:
                rot[blk * 64 + m_ - 32, blk * 64 + m_] = 1.0
    bo2 = np.zeros((128, 2), np.float32)
    bc2 = np.zeros((2, 128), np.float32)
    for p in range(128):
        bo2[p, p // 64] = 1.0
        bc2[p // 64, p] = 1.0
    mgen = _mk_masks(False)
    m0sp = _mk_masks(True)
    TH, TQK = _T + 513, _T + 512

    def attn_tabs(q):
        t0 = q * _T
        cq, sq_, scq = tables(np.arange(t0, t0 + _T, dtype=np.float32))
        ck, sk_, sck = tables(np.arange(t0 - 512, t0 + _T, dtype=np.float32))
        cosq8 = (8 * cq * scq).T
        sinq8 = (8 * sq_ * scq).T
        cosk = (ck / sck).T
        sink = (sk_ / sck).T
        tabq = np.concatenate([np.tile(cosq8, (2, 1)).reshape(128, _T),
                               np.tile(sinq8, (2, 1)).reshape(128, _T)],
                              1).astype(ml_dtypes.bfloat16)
        tabk = np.concatenate([np.tile(cosk, (2, 1)).reshape(128, TQK),
                               np.tile(sink, (2, 1)).reshape(128, TQK)],
                              1).astype(ml_dtypes.bfloat16)
        return tabq, tabk
    tabs = [attn_tabs(q) for q in range(4)]

    def run_attn(layer, first, x_or_xp, parts_arr=None):
        nc = bass.Bass("TRN2", target_bir_lowering=False)
        emit_attn(nc, _T, first=first)
        qsv = np.stack([np.tile(ins["L0_ffn_q_scale"][layer], 2),
                        np.tile(ins["L0_ffn_k_scale"][layer], 2)], 1
                       ).astype(np.float32)
        maps = []
        for c in _CORES:
            b, q = c // 4, c % 4
            mp = {
                "wqkv": ins["L0_ffn_wqkv"][layer],
                "wo": ins["L0_ffn_wo"][layer],
                "qsd": qsv,
                "tabq": tabs[q][0], "tabk": tabs[q][1],
                "mgen": mgen, "m0": m0sp if q == 0 else mgen,
                "identd": ident, "rotd": rot, "bo2d": bo2, "bc2d": bc2,
                "onesd": ones, "constsd": consts,
            }
            if first:
                mp["i_x"] = _sliceT(x_or_xp[b], q * _T - 513, TH)
                pa = np.zeros((8, 1024, TH), np.float32)
                start = q * _T - 513
                s0 = max(0, start)
                pad = s0 - start
                for jj in range(8):
                    pa[jj, :, pad:] = parts_arr[jj, :, b, s0:start + TH]
                mp["i_parts"] = pa
            else:
                mp["i_xp"] = _sliceT(x_or_xp[b], q * _T - 513, TH)
            maps.append(mp)
        return run_bass_kernel_spmd(nc, maps, core_ids=_CORES)

    def gather_T(res, key, width=_T):
        out = np.zeros((2, _NG, 1024), np.float32)
        for c in _CORES:
            b, q = c // 4, c % 4
            out[b, q * _T:(q + 1) * _T] = res.results[c][key].T
        return out

    r3 = run_attn(0, True, x0, parts)
    x1 = gather_T(r3, "o_x1")
    xa0 = gather_T(r3, "o_xa")

    def run_geglu(layer, last, xa, x1_arr=None):
        nc = bass.Bass("TRN2", target_bir_lowering=False)
        emit_geglu(nc, _T, last=last)
        HBW = [128] * 21 + [42]
        maps = []
        for c in _CORES:
            b, q = c // 4, c % 4
            mp = {
                "i_xa": _sliceT(xa[b], q * _T - 1, _T + 1),
                "w1": ins["L0_ffn_w1"][layer], "w2": ins["L0_ffn_w2"][layer],
                "b1ad": _packr(ins["L0_ffn_b1"][layer][:2730], HBW),
                "b1gd": _packr(ins["L0_ffn_b1"][layer][2730:], HBW),
                "lgd": pack_vec(ins["L0_ffn_ln_g"][layer], 8),
                "lbd": pack_vec(ins["L0_ffn_ln_b"][layer], 8),
                "b2d": pack_vec(ins["L0_ffn_b2"][layer], 8),
                "onesd": ones, "constsd": consts,
            }
            if last:
                mp["i_x1"] = np.ascontiguousarray(
                    x1_arr[b, q * _T:(q + 1) * _T].T)
            maps.append(mp)
        return run_bass_kernel_spmd(nc, maps, core_ids=_CORES)

    r4 = run_geglu(0, False, xa0)
    xp1 = gather_T(r4, "o_out")

    r5 = run_attn(1, False, xp1)
    xa1 = gather_T(r5, "o_xa")
    r6 = run_geglu(1, True, xa1, x1)
    x2 = gather_T(r6, "o_out")

    nc = bass.Bass("TRN2", target_bir_lowering=False)
    emit_rwkv_a(nc, _T)
    tm = ins["L1_attn_time_mix"][0, 0]
    td_rep = np.repeat(ins["L1_attn_time_decay"][:, 0], 64)
    tf_rep = np.repeat(ins["L1_attn_time_first"][:, 0], 64)
    wkT = np.ascontiguousarray(ins["L1_attn_Wk"].T)
    wvT = np.ascontiguousarray(ins["L1_attn_Wv"].T)
    wrT = np.ascontiguousarray(ins["L1_attn_Wr"].T)
    maps = []
    for c in _CORES:
        b, q = c // 4, c % 4
        maps.append({
            "x2T": _sliceT(x2[b], q * _T - 1, _T + 1),
            "wkT": wkT, "wvT": wvT, "wrT": wrT,
            "tmd": pack_vec(np.asarray(tm, np.float32), 8),
            "tdd": pack_vec(td_rep, 8),
            "onesd": ones, "constsd": consts,
        })
    r7 = run_bass_kernel_spmd(nc, maps, core_ids=_CORES)

    nc = bass.Bass("TRN2", target_bir_lowering=False)
    emit_rwkv_b(nc, _T)
    woT = np.ascontiguousarray(ins["L1_attn_Wo"].T)
    maps = []
    for c in _CORES:
        b, q = c // 4, c % 4
        M_all = np.zeros((128, 64), np.float32)
        for jq in range(4):
            M_all[:, 16 * jq:16 * jq + 16] = r7.results[b * 4 + jq]["o_M"]
        sel = np.zeros((128, 4), np.float32)
        sel[:, :q] = 1.0
        maps.append({
            "i_kT": r7.results[c]["o_kT"], "i_kvT": r7.results[c]["o_kvT"],
            "i_srT": r7.results[c]["o_srT"], "i_SkT": r7.results[c]["o_SkT"],
            "i_SkvT": r7.results[c]["o_SkvT"],
            "i_Mall": M_all, "i_sel": sel,
            "x2T": np.ascontiguousarray(x2[b, q * _T:(q + 1) * _T].T),
            "woT": woT, "tdd": pack_vec(td_rep, 8), "tfd": pack_vec(tf_rep, 8),
            "onesd": ones, "constsd": consts,
            "tbase": np.zeros((1, 4), np.float32),
        })
    r8 = run_bass_kernel_spmd(nc, maps, core_ids=_CORES)
    x3 = gather_T(r8, "x3T")

    nc = bass.Bass("TRN2", target_bir_lowering=False)
    emit_l1ffn(nc, _T)
    maps = []
    for c in _CORES:
        b, q = c // 4, c % 4
        maps.append({
            "x3T": np.ascontiguousarray(x3[b, q * _T:(q + 1) * _T].T),
            "w1e": ins["L1_ffn_w1"][0], "w2e": ins["L1_ffn_w2"][0],
            "b1d": pack_vec(ins["L1_ffn_b1"][0], 32),
            "b2d": pack_vec(ins["L1_ffn_b2"][0], 8),
            "gmd": pack_vec(ins["L1_ffn_gamma"], 8),
            "btd": pack_vec(ins["L1_ffn_beta"], 8),
            "onesd": ones, "constsd": consts,
        })
    r9 = run_bass_kernel_spmd(nc, maps, core_ids=_CORES)
    x4 = gather_T(r9, "x4T")
    return x4
